# version 12
# baseline (speedup 1.0000x reference)
import contextlib
import numpy as np
import ml_dtypes
import concourse.bass as bass
import concourse.mybir as mybir
from concourse.bass_utils import run_bass_kernel_spmd

F32 = mybir.dt.float32
BF16 = mybir.dt.bfloat16
AF = mybir.ActivationFunctionType
ALU = mybir.AluOpType
AX = mybir.AxisListType

D = 1024
HID = 4096
E = 8
KC = D // 128
MC = HID // 128
NCORES = 8
NPAIR = 8
EPS = 1e-5
SEG = 512


class Prog:
    ENGS = ("tensor", "scalar", "vector", "gpsimd", "sync")

    def __init__(self):
        self.streams = {e: [] for e in self.ENGS}

    def wait(self, eng, sem, count):
        if count > 0:
            self.streams[eng].append(("w", sem, count))

    def op(self, eng, fn, inc=None):
        self.streams[eng].append(("o", fn, inc or []))

    def check(self):
        sems, pos = {}, {e: 0 for e in self.ENGS}
        progress = True
        while progress:
            progress = False
            for e in self.ENGS:
                while pos[e] < len(self.streams[e]):
                    it = self.streams[e][pos[e]]
                    if it[0] == "w":
                        if sems.get(it[1].name, 0) < it[2]:
                            break
                    else:
                        for sem, k in it[2]:
                            sems[sem.name] = sems.get(sem.name, 0) + k
                    pos[e] += 1
                    progress = True
        stuck = [(e, pos[e], len(self.streams[e])) for e in self.ENGS
                 if pos[e] < len(self.streams[e])]
        if stuck:
            parts = []
            for e, pp, nn_ in stuck:
                it = self.streams[e][pp]
                parts.append(f"{e}@{pp}/{nn_} waits {it[1].name}>={it[2]} "
                             f"(have {sems.get(it[1].name, 0)})")
            raise RuntimeError("Prog deadlock: " + "; ".join(parts))

    def emit(self, eng_name, eng):
        for it in self.streams[eng_name]:
            if it[0] == "w":
                eng.wait_ge(it[1], it[2])
            else:
                ins = it[1](eng)
                for sem, k in it[2]:
                    ins.then_inc(sem, k)


def build_program(seq):
    NT = seq // 512
    QT = seq // 128
    TC = seq // 128

    nc = bass.Bass("TRN2", target_bir_lowering=False)
    dr = lambda name, s, d, k: nc.dram_tensor(name, s, d, kind=k).ap()

    x_d = dr("x", [D, seq], F32, "ExternalInput")
    out_d = dr("out", [D, seq], F32, "ExternalOutput")
    w1_d = dr("w1r", [E, 128, MC, KC, 128], F32, "ExternalInput")
    w2_d = dr("w2r", [E, 128, MC, D], F32, "ExternalInput")
    b1_d = dr("b1r", [128, E * MC], F32, "ExternalInput")
    b2_d = dr("b2r", [8, D], F32, "ExternalInput")
    gwT_d = dr("gwT", [KC, 128, 8], F32, "ExternalInput")
    gbb_d = dr("gbb", [128, 8], F32, "ExternalInput")
    wq_d = dr("wqr", [NPAIR, 128, KC, 128], F32, "ExternalInput")
    wk_d = dr("wkr", [NPAIR, 128, KC, 128], F32, "ExternalInput")
    wv_d = dr("wvr", [NPAIR, 128, KC, 128], F32, "ExternalInput")
    wo_d = dr("wor", [NPAIR, 128, D], F32, "ExternalInput")
    bq_d = dr("bqr", [128, NPAIR], F32, "ExternalInput")
    bk_d = dr("bkr", [128, NPAIR], F32, "ExternalInput")
    bvb_d = dr("bvbr", [128, NPAIR * 128], BF16, "ExternalInput")
    bo_d = dr("bor", [128, KC], F32, "ExternalInput")
    gam_d = dr("gamr", [128, KC], F32, "ExternalInput")
    bet_d = dr("betr", [128, KC], F32, "ExternalInput")
    iden_d = dr("idenb", [128, 128], BF16, "ExternalInput")
    mask_d = dr("maskb", [128, 128], BF16, "ExternalInput")
    ones_d = dr("onesb", [128, 1], BF16, "ExternalInput")
    onesf_d = dr("onesf", [1, 128], F32, "ExternalInput")
    esel_d = dr("eselb", [8, E * 128], BF16, "ExternalInput")
    xlns_d = dr("xlns", [D, seq], BF16, "Internal")
    os_d = dr("osp", [NPAIR, 128, seq], BF16, "Internal")

    st = contextlib.ExitStack()
    sb = lambda name, s, d: st.enter_context(nc.sbuf_tensor(name, s, d))

    pool1 = sb("pool1", [128, 32768], BF16)
    pool2 = sb("pool2", [128, 32768], BF16)
    pool3 = sb("pool3", [128, 16384], BF16)
    ft = sb("ft", [128, 8192], F32)
    w8 = sb("w8", [8, seq], BF16)
    b1t = sb("b1t", [128, E * MC], F32)
    b2t = sb("b2t", [8, D], BF16)
    gwT = sb("gwTs", [128, KC * 8], BF16)
    gbb = sb("gbbs", [128, 8], F32)
    bqs = sb("bqs", [128, NPAIR], F32)
    bks = sb("bks", [128, NPAIR], F32)
    bvb = sb("bvbs", [128, NPAIR * 128], BF16)
    bos = sb("boss", [128, KC], F32)
    gams = sb("gams", [128, KC], F32)
    bets = sb("betss", [128, KC], F32)
    iden = sb("idens", [128, 128], BF16)
    maskb = sb("maskbs", [128, 128], BF16)
    onesb = sb("onesbs", [128, 1], BF16)
    onesf = sb("onesfs", [1, 128], F32)
    stat = sb("stat", [128, 32], F32)
    esel = ft[:].bitcast(BF16)[0:8, 6144:6144 + E * 128]

    FT_STG = (0, 512)
    FT_WB = (1024, 1536)
    FT_ROW = 2048
    FT_STX = 3584 + 512
    FT_STX = 3840
    FT_X = 4096
    FT_Y = 0
    rowsB = ft
    stx = ft

    pp = [st.enter_context(nc.psum_tensor(f"pp{i}", [128, 1024], F32))
          for i in range(4)]
    p = [pp[i // 2][:, (i % 2) * 512:(i % 2 + 1) * 512] for i in range(8)]

    sem = lambda name: st.enter_context(nc.semaphore(name))
    g_w1 = sem("g_w1"); g_w2 = sem("g_w2"); g_xb = sem("g_xb")
    g_acca = sem("g_acca"); g_accb = sem("g_accb")
    g_att0 = sem("g_att0"); g_att1 = sem("g_att1"); g_wo = sem("g_wo"); g_or = sem("g_or")
    g_pre = sem("g_pre")
    h_xlw = sem("h_xlw"); h_xlr0 = sem("h_xlr0"); h_xlr1 = sem("h_xlr1"); h_ow = sem("h_ow")
    h_o0 = sem("h_o0"); h_o1 = sem("h_o1"); h_o2 = sem("h_o2"); h_o3 = sem("h_o3")
    h_xw = sem("h_xw"); h_pre = sem("h_pre")
    c_h = sem("c_h"); c_gel = sem("c_gel"); c_o = sem("c_o"); c_ev = sem("c_ev")
    c_wb = sem("c_wb"); c_wbe = sem("c_wbe"); c_ln = sem("c_ln")
    c_st = sem("c_st"); c_g = sem("c_g"); c_mm = sem("c_mm")
    c_nm = sem("c_nm"); c_exp = sem("c_exp"); c_psc = sem("c_psc")
    c_tp = sem("c_tp"); c_pte = sem("c_pte"); c_avb = sem("c_avb")
    c_oev = sem("c_oev")

    P = Prog()
    nstage = [0]
    n = {}
    for s_ in (g_w1, g_w2, g_xb, g_acca, g_accb, g_att0, g_att1, g_wo, g_or,
               g_pre, h_xlw, h_xlr0, h_xlr1, h_ow, h_o0, h_o1, h_o2, h_o3,
               h_xw, h_pre, c_h, c_gel,
               c_o, c_ev, c_wb, c_wbe, c_ln, c_st, c_g, c_mm, c_nm, c_exp,
               c_psc, c_tp, c_pte, c_avb, c_oev):
        n[s_.name] = 0

    def gdma(dst, src, s, accum=False):
        if accum:
            P.op("gpsimd", lambda g, dst=dst, src=src: g.dma_start(
                out=dst, in_=src, accum_op=ALU.add), inc=[(s, 16)])
        else:
            P.op("gpsimd", lambda g, dst=dst, src=src: g.dma_start(
                out=dst, in_=src), inc=[(s, 16)])
        n[s.name] += 16

    def sdma(dst, src, s):
        P.op("sync", lambda sy, dst=dst, src=src: sy.dma_start(out=dst, in_=src),
             inc=[(s, 16)])
        n[s.name] += 16

    def top(fn, s=None):
        inc = [(s, 1)] if s is not None else []
        P.op("tensor", fn, inc=inc)
        if s is not None:
            n[s.name] += 1

    def vop(fn, s=None):
        inc = [(s, 1)] if s is not None else []
        P.op("vector", fn, inc=inc)
        if s is not None:
            n[s.name] += 1

    def vch(fn):
        P.wait("vector", c_st, n["c_st"])
        P.op("vector", fn, inc=[(c_st, 1)])
        n["c_st"] += 1

    def sch(fn):
        P.wait("scalar", c_st, n["c_st"])
        P.op("scalar", fn, inc=[(c_st, 1)])
        n["c_st"] += 1

    out_v = out_d.rearrange("(k p) t -> p k t", p=128)
    xlns_v = xlns_d.rearrange("(k p) t -> p k t", p=128)

    for dst, src in ((b1t, b1_d), (gbb, gbb_d), (bqs, bq_d), (bks, bk_d),
                     (bvb, bvb_d), (bos, bo_d), (gams, gam_d), (bets, bet_d),
                     (onesf, onesf_d), (b2t, b2_d), (iden, iden_d),
                     (maskb, mask_d), (onesb, ones_d), (esel, esel_d)):
        gdma(dst[:], src[:], g_pre)
    gdma(gwT[:].rearrange("p (k j) -> p k j", k=KC),
         gwT_d.rearrange("k i j -> i k j"), g_pre)
    sdma(out_d[:], x_d[:], h_pre)
    PRE_G = n["g_pre"]
    PRE_H = n["h_pre"]
    P.wait("tensor", g_pre, PRE_G)
    P.wait("scalar", g_pre, PRE_G)
    P.wait("vector", g_pre, PRE_G)

    def ln_stats_and_rstd():
        r_mu = rowsB[0:1, 1536:2048]
        r_rs = rowsB[0:1, 2048:2560]
        r_t1 = rowsB[0:1, 3584:4096]
        vch(lambda v: v.tensor_scalar(out=r_mu, in0=p[5][0:1, :],
                                      scalar1=1.0 / D, scalar2=None, op0=ALU.mult))
        vch(lambda v: v.tensor_scalar(out=r_rs, in0=p[6][0:1, :],
                                      scalar1=1.0 / D, scalar2=None, op0=ALU.mult))
        vch(lambda v: v.tensor_tensor(out=r_t1, in0=r_mu, in1=r_mu, op=ALU.mult))
        vch(lambda v: v.tensor_tensor(out=r_rs, in0=r_rs, in1=r_t1, op=ALU.subtract))
        vch(lambda v: v.tensor_scalar(out=r_rs, in0=r_rs, scalar1=EPS,
                                      scalar2=None, op0=ALU.add))
        sch(lambda s: s.activation(out=r_rs, in_=r_rs, func=AF.Sqrt))
        vch(lambda v: v.reciprocal(out=r_rs, in_=r_rs))
        return r_mu, r_rs

    def stats_mms(src_a, src_b):
        for k in range(KC):
            top(lambda t, k=k: t.matmul(
                p[5][0:1, :], onesb[:], src_a[:, k * 512:(k + 1) * 512],
                start=(k == 0), stop=(k == KC - 1)))
            top(lambda t, k=k: t.matmul(
                p[6][0:1, :], onesb[:], src_b[:, k * 512:(k + 1) * 512],
                start=(k == 0), stop=(k == KC - 1)),
                c_mm if k == KC - 1 else None)

    def bcast_mms(r_mu, r_rs):
        top(lambda t: t.matmul(p[5][:, :], onesf[:], r_mu, start=True, stop=True))
        top(lambda t: t.matmul(p[6][:, :], onesf[:], r_rs, start=True, stop=True),
            c_mm)

    def ffn_call(call, xdep_sem, xdep_base, xdep_step):
        xb = pool3[:, 0:4096]
        xsq = pool3[:, 4096:8192]
        xlt = pool3[:, 8192:12288]
        gtile = pool3[:, 12288:12296]
        ev_hist = []

        for ts in range(NT):
            P.wait("gpsimd", c_st, n["c_st"])
            P.wait("gpsimd", xdep_sem, xdep_base + xdep_step * (ts + 1))
            gdma(xb.rearrange("p (k t) -> p k t", k=KC),
                 out_v[:, :, ts * 512:(ts + 1) * 512], g_xb)
            P.wait("scalar", g_xb, n["g_xb"])
            P.wait("scalar", c_mm, n["c_mm"])
            P.op("scalar", lambda s: s.activation(out=xsq, in_=xb, func=AF.Square),
                 inc=[(c_ln, 1)]); n["c_ln"] += 1
            P.wait("tensor", c_ln, n["c_ln"])
            P.wait("tensor", c_st, n["c_st"])
            stats_mms(xb, xsq)
            P.wait("vector", c_mm, n["c_mm"])
            r_mu, r_rs = ln_stats_and_rstd()
            P.wait("tensor", c_st, n["c_st"])
            bcast_mms(r_mu, r_rs)
            P.wait("vector", c_mm, n["c_mm"])
            P.wait("vector", c_g, n["c_g"])
            P.wait("vector", h_xlw, n["h_xlw"])
            for k in range(KC):
                stg = ft[:, 3584:3584 + 512]
                vch(lambda v, k=k, stg=stg: v.tensor_tensor(
                    out=stg, in0=xb[:, k * 512:(k + 1) * 512], in1=p[5][:, :],
                    op=ALU.subtract))
                vch(lambda v, k=k, stg=stg: v.tensor_tensor(
                    out=xlt[:, k * 512:(k + 1) * 512], in0=stg, in1=p[6][:, :],
                    op=ALU.mult))
            P.wait("sync", c_st, n["c_st"])
            P.wait("sync", h_xlr0, n["h_xlr0"])
            P.wait("sync", h_xlr1, n["h_xlr1"])
            sdma(xlns_v[:, :, ts * 512:(ts + 1) * 512],
                 xlt.rearrange("p (k t) -> p k t", k=KC), h_xlw)
            for g4 in range(4):
                tok0 = g4 * 128
                P.wait("tensor", c_g, n["c_g"])
                P.wait("tensor", c_pte, n["c_pte"])
                P.wait("tensor", c_wbe, n["c_wbe"])
                P.wait("tensor", c_st, n["c_st"])
                for k in range(KC):
                    top(lambda t, k=k, tok0=tok0: t.matmul(
                        p[4][:, 0:8],
                        xlt[:, k * 512 + tok0:k * 512 + tok0 + 128],
                        gwT[:, k * 8:(k + 1) * 8],
                        start=(k == 0), stop=(k == KC - 1)),
                        c_mm if k == KC - 1 else None)
                gg = stat[:, 0:8]; m1 = stat[:, 8:9]; m2 = stat[:, 9:10]
                msk1 = stat[:, 10:18]; tmp8 = stat[:, 18:26]
                e21 = stat[:, 26:27]; wA = stat[:, 27:28]; wB = stat[:, 28:29]
                P.wait("vector", c_mm, n["c_mm"])
                vch(lambda v: v.tensor_tensor(out=gg, in0=p[4][:, 0:8],
                                              in1=gbb[:], op=ALU.add))
                vch(lambda v: v.reduce_max(out=m1, in_=gg, axis=AX.X))
                vch(lambda v: v.tensor_scalar(out=msk1, in0=gg, scalar1=m1,
                                              scalar2=None, op0=ALU.is_ge))
                vch(lambda v: v.tensor_scalar(out=tmp8, in0=msk1, scalar1=1e30,
                                              scalar2=None, op0=ALU.mult))
                vch(lambda v: v.tensor_tensor(out=tmp8, in0=gg, in1=tmp8,
                                              op=ALU.subtract))
                vch(lambda v: v.reduce_max(out=m2, in_=tmp8, axis=AX.X))
                vch(lambda v: v.tensor_tensor(out=e21, in0=m2, in1=m1,
                                              op=ALU.subtract))
                sch(lambda s: s.activation(out=e21, in_=e21, func=AF.Exp))
                vch(lambda v: v.tensor_scalar(out=wA, in0=e21, scalar1=1.0,
                                              scalar2=None, op0=ALU.add))
                vch(lambda v: v.reciprocal(out=wA, in_=wA))
                vch(lambda v: v.tensor_scalar(out=wB, in0=wA, scalar1=-1.0,
                                              scalar2=1.0, op0=ALU.mult,
                                              op1=ALU.add))
                vch(lambda v: v.tensor_scalar(out=tmp8, in0=tmp8, scalar1=m2,
                                              scalar2=None, op0=ALU.is_ge))
                vch(lambda v: v.tensor_scalar(out=tmp8, in0=tmp8, scalar1=wB,
                                              scalar2=None, op0=ALU.mult))
                vch(lambda v: v.tensor_scalar(out=msk1, in0=msk1, scalar1=wA,
                                              scalar2=None, op0=ALU.mult))
                vch(lambda v: v.tensor_tensor(out=msk1, in0=msk1, in1=tmp8,
                                              op=ALU.add))
                vch(lambda v: v.tensor_scalar(out=gtile, in0=msk1, scalar1=0.5,
                                              scalar2=None, op0=ALU.mult))
                p4b = p[4][:].bitcast(BF16)
                P.wait("tensor", c_st, n["c_st"])
                top(lambda t, p4b=p4b: t.matmul(
                    p4b[0:8, 128:256], gtile, iden[:], start=True, stop=True,
                    is_transpose=True), c_mm)
                P.wait("vector", c_mm, n["c_mm"])
                vop(lambda v, ts=ts, g4=g4, p4b=p4b: v.tensor_copy(
                    out=w8[0:8, ts * 512 + g4 * 128:ts * 512 + (g4 + 1) * 128],
                    in_=p4b[0:8, 128:256]), c_g)
            for d in range(KC):
                pb = p[d % 2]
                P.wait("tensor", c_ev, max(0, n["c_ev"] - 1))
                P.wait("tensor", c_g, n["c_g"])
                top(lambda t, d=d, ts=ts, pb=pb: t.matmul(
                    pb[:, :], b2t[0:8, d * 128:(d + 1) * 128],
                    w8[0:8, ts * 512:(ts + 1) * 512], start=True, stop=True),
                    c_mm)
                sg = nstage[0] % 2
                nstage[0] += 1
                sacc = g_acca if sg == 0 else g_accb
                stg = ft[:, FT_STG[sg]:FT_STG[sg] + 512]
                P.wait("vector", c_mm, n["c_mm"])
                P.wait("vector", sacc, n[sacc.name])
                vop(lambda v, pb=pb, stg=stg: v.tensor_copy(out=stg, in_=pb[:, :]),
                    c_ev)
                P.wait("gpsimd", c_ev, n["c_ev"])
                gdma(out_d.rearrange("(c p) t -> c p t", p=128)[d, :, ts * 512:(ts + 1) * 512],
                     stg, sacc, accum=True)

        LN_ACCA = n["g_acca"]
        LN_ACCB = n["g_accb"]
        XLW_ALL = n["h_xlw"]
        W8_DONE = n["c_g"]

        xl2 = ft[:, FT_X:FT_X + 4096].bitcast(BF16)
        h_base = n["c_h"]; o_base = n["c_o"]; gel_base = n["c_gel"]
        gw1_base = n["g_w1"]; gw2_base = n["g_w2"]
        xlr_bases = (n["h_xlr0"], n["h_xlr1"])

        def load_w1(e):
            if e > 0:
                P.wait("gpsimd", c_h, h_base + ((e - 1) * NT + NT) * MC)
            gdma(pool1[:, :].rearrange("p (m k j) -> p m k j", m=MC, k=KC),
                 w1_d[e], g_w1)

        def load_w2(e):
            if e > 0:
                P.wait("gpsimd", c_o, o_base + e * NT * KC)
            gdma(pool2[:, :].rearrange("p (m dd) -> p m dd", m=MC),
                 w2_d[e], g_w2)

        P.wait("gpsimd", c_ev, n["c_ev"])
        load_w1(0)
        load_w2(0)

        P.wait("tensor", c_ev, n["c_ev"])
        for e in range(E):
            for ts in range(NT):
                idx = e * NT + ts
                sxlr = h_xlr0 if idx % 2 == 0 else h_xlr1
                if idx >= 2:
                    P.wait("sync", c_h, h_base + (idx - 1) * MC)
                P.wait("sync", h_xlw, XLW_ALL)
                sdma(xl2[:, (idx % 2) * 4096:(idx % 2 + 1) * 4096].rearrange(
                        "p (k t) -> p k t", k=KC),
                     xlns_v[:, :, ts * 512:(ts + 1) * 512], sxlr)
                xt = xl2[:, (idx % 2) * 4096:(idx % 2 + 1) * 4096]

                wbslot = idx % 2
                wbv = ft[:, FT_WB[wbslot]:FT_WB[wbslot] + 512]
                P.wait("tensor", c_wbe, max(0, n["c_wbe"] - 1))
                P.wait("tensor", c_g, W8_DONE)
                P.wait("tensor", c_pte, n["c_pte"])
                top(lambda t, e=e, ts=ts: t.matmul(
                    p[4][:, :], esel[0:8, e * 128:(e + 1) * 128],
                    w8[0:8, ts * 512:(ts + 1) * 512], start=True, stop=True),
                    c_wb)
                P.wait("vector", c_wb, n["c_wb"])
                if idx >= 2:
                    P.wait("vector", c_ev, ev_hist[idx - 2])
                vop(lambda v, wbv=wbv: v.tensor_copy(out=wbv, in_=p[4][:, :]),
                    c_wbe)

                sxlr = h_xlr0 if idx % 2 == 0 else h_xlr1
                P.wait("tensor", sxlr, xlr_bases[idx % 2] + 16 * (idx // 2 + 1))
                for m in range(MC):
                    gi = idx * MC + m
                    P.wait("tensor", g_w1, gw1_base + 16 * (e + 1))
                    if gi >= 2:
                        P.wait("tensor", c_gel, gel_base + gi - 1)
                    for k in range(KC):
                        top(lambda t, m=m, k=k, xt=xt: t.matmul(
                            p[m % 2][:, :],
                            pool1[:, m * 1024 + k * 128:m * 1024 + (k + 1) * 128],
                            xt[:, k * 512:(k + 1) * 512],
                            start=(k == 0), stop=(k == KC - 1)),
                            c_h if k == KC - 1 else None)
                    P.wait("scalar", c_h, n["c_h"])
                    if idx > 0:
                        P.wait("scalar", c_o, o_base + idx * KC)
                    P.op("scalar", lambda s, m=m, e=e: s.activation(
                        out=pool3[:, m * 512:(m + 1) * 512], in_=p[m % 2][:, :],
                        func=AF.Gelu, bias=b1t[:, e * MC + m:e * MC + m + 1]),
                        inc=[(c_gel, 1)]); n["c_gel"] += 1
                P.wait("tensor", c_gel, gel_base + (idx + 1) * MC)
                P.wait("tensor", g_w2, gw2_base + 16 * (e + 1))
                P.wait("tensor", c_wbe, n["c_wbe"])
                for d in range(KC):
                    P.wait("tensor", c_ev, max(0, n["c_ev"] - 1))
                    for m in range(MC):
                        top(lambda t, d=d, m=m: t.matmul(
                            p[2 + d % 2][:, :],
                            pool2[:, m * 1024 + d * 128:m * 1024 + (d + 1) * 128],
                            pool3[:, m * 512:(m + 1) * 512],
                            start=(m == 0), stop=(m == MC - 1)),
                            c_o if m == MC - 1 else None)
                    sg = nstage[0] % 2
                    nstage[0] += 1
                    sacc = g_acca if sg == 0 else g_accb
                    stg = ft[:, FT_STG[sg]:FT_STG[sg] + 512]
                    P.wait("vector", c_o, n["c_o"])
                    P.wait("vector", sacc, n[sacc.name])
                    vop(lambda v, d=d, stg=stg, wbv=wbv: v.tensor_tensor(
                        out=stg, in0=p[2 + d % 2][:, :], in1=wbv, op=ALU.mult),
                        c_ev)
                    P.wait("gpsimd", c_ev, n["c_ev"])
                    if e == 0 and ts == 0 and d == 0:
                        P.wait("gpsimd", g_acca, LN_ACCA)
                        P.wait("gpsimd", g_accb, LN_ACCB)
                    gdma(out_d.rearrange("(c p) t -> c p t", p=128)[d, :, ts * 512:(ts + 1) * 512],
                         stg, sacc, accum=True)
                ev_hist.append(n["c_ev"])
            if e + 1 < E:
                load_w1(e + 1)
                load_w2(e + 1)

    def attention():
        ACC1A = n["g_acca"]
        ACC1B = n["g_accb"]
        xb = pool3[:, 0:4096]
        xsq = pool3[:, 4096:8192]
        for ts in range(NT):
            P.wait("gpsimd", c_st, n["c_st"])
            P.wait("gpsimd", g_acca, ACC1A)
            P.wait("gpsimd", g_accb, ACC1B)
            gdma(xb.rearrange("p (k t) -> p k t", k=KC),
                 out_v[:, :, ts * 512:(ts + 1) * 512], g_xb)
            P.wait("scalar", g_xb, n["g_xb"])
            P.wait("scalar", c_mm, n["c_mm"])
            P.op("scalar", lambda s: s.activation(out=xsq, in_=xb, func=AF.Square),
                 inc=[(c_ln, 1)]); n["c_ln"] += 1
            P.wait("tensor", c_ln, n["c_ln"])
            P.wait("tensor", c_st, n["c_st"])
            stats_mms(xb, xsq)
            P.wait("vector", c_mm, n["c_mm"])
            r_mu, r_rs = ln_stats_and_rstd()
            P.wait("tensor", c_st, n["c_st"])
            bcast_mms(r_mu, r_rs)
            P.wait("vector", c_mm, n["c_mm"])
            for k in range(KC):
                stg = ft[:, 3584:3584 + 512]
                vch(lambda v, k=k, stg=stg: v.tensor_tensor(
                    out=stg, in0=xb[:, k * 512:(k + 1) * 512], in1=p[5][:, :],
                    op=ALU.subtract))
                vch(lambda v, k=k, ts=ts, stg=stg: v.tensor_tensor(
                    out=pool1[:, k * seq + ts * 512:k * seq + (ts + 1) * 512],
                    in0=stg, in1=p[6][:, :], op=ALU.mult))
        XLN_DONE = n["c_st"]

        S_ = seq
        B_QT = (0, 12288)
        B_KT = (4096, 16384)
        B_V = (8192, 20480)
        C_PT = 24576
        WSLOT = 26624
        ostg = ft[:].bitcast(BF16)[:, 5400:5912]
        segctr = [0]
        exp_hist = []

        def Pbuf(m, hh):
            return pool3, m * 8192 + hh * 4096

        def stats_tiles(m):
            b = 2560 + m * 70
            return dict(nmx=ft[:, b:b + 16], ssm=ft[:, b + 16:b + 32],
                        negm=ft[:, b + 32:b + 34], ddv=ft[:, b + 34:b + 50],
                        ccv=ft[:, b + 50:b + 66], Z2=ft[:, b + 66:b + 68],
                        rz2=ft[:, b + 68:b + 70])
        STM = (stats_tiles(0), stats_tiles(1))

        def load_pair_w(pr, gate_ev):
            slot = pr % 2
            satt = g_att0 if slot == 0 else g_att1
            P.wait("gpsimd", c_ev, gate_ev)
            for wi, wd in ((0, wq_d), (1, wk_d), (2, wv_d)):
                gdma(pool2[:, WSLOT + (slot * 3 + wi) * 1024:
                           WSLOT + (slot * 3 + wi + 1) * 1024].rearrange(
                               "p (k j) -> p k j", k=KC), wd[pr], satt)

        load_pair_w(0, n["c_ev"])
        load_pair_w(1, n["c_ev"])
        qkv_ev = {}
        qgrp = [0]
        qev_hist = []
        ost = [0]
        tb = [0]
        h_o = (h_o0, h_o1, h_o2, h_o3)
        ouse = [0, 0, 0, 0]

        P.wait("tensor", c_wbe, n["c_wbe"])

        def emit_qkv(m, pr):
            P.wait("tensor", c_exp, n["c_exp"])
            slot = pr % 2
            wq_s = pool2[:, WSLOT + (slot * 3 + 0) * 1024:WSLOT + (slot * 3 + 1) * 1024]
            wk_s = pool2[:, WSLOT + (slot * 3 + 1) * 1024:WSLOT + (slot * 3 + 2) * 1024]
            wv_s = pool2[:, WSLOT + (slot * 3 + 2) * 1024:WSLOT + (slot * 3 + 3) * 1024]
            satt = g_att0 if slot == 0 else g_att1
            P.wait("tensor", satt, 16 * 3 * (pr // 2 + 1))
            P.wait("tensor", c_st, XLN_DONE)
            for which, wsb, bias in (("q", wq_s, bqs), ("k", wk_s, bks)):
                for t5 in range(NT):
                    pq = p[qgrp[0] % 2]
                    if len(qev_hist) >= 2:
                        P.wait("tensor", c_ev, qev_hist[-2])
                    for k in range(KC):
                        top(lambda t, k=k, t5=t5, wsb=wsb, pq=pq: t.matmul(
                            pq[:, :], wsb[:, k * 128:(k + 1) * 128],
                            pool1[:, k * seq + t5 * 512:k * seq + (t5 + 1) * 512],
                            start=(k == 0), stop=(k == KC - 1)),
                            c_mm if k == KC - 1 else None)
                    dstc = (B_QT[m] if which == "q" else B_KT[m]) + t5 * 512
                    P.wait("vector", c_mm, n["c_mm"])
                    if which == "q":
                        vop(lambda v, dstc=dstc, bias=bias, pr=pr, pq=pq:
                            v.tensor_scalar(
                                out=pool2[:, dstc:dstc + 512], in0=pq[:, :],
                                scalar1=bias[:, pr:pr + 1], scalar2=0.125,
                                op0=ALU.add, op1=ALU.mult), c_ev)
                    else:
                        vop(lambda v, dstc=dstc, bias=bias, pr=pr, pq=pq:
                            v.tensor_scalar(
                                out=pool2[:, dstc:dstc + 512], in0=pq[:, :],
                                scalar1=bias[:, pr:pr + 1], scalar2=None,
                                op0=ALU.add), c_ev)
                    qgrp[0] += 1
                    qev_hist.append(n["c_ev"])
            for tc in range(TC):
                pq = p[qgrp[0] % 2]
                if len(qev_hist) >= 2:
                    P.wait("tensor", c_ev, qev_hist[-2])
                for k in range(KC):
                    top(lambda t, k=k, tc=tc, wv_s=wv_s, pq=pq: t.matmul(
                        pq[:, 0:128],
                        pool1[:, k * seq + tc * 128:k * seq + (tc + 1) * 128],
                        wv_s[:, k * 128:(k + 1) * 128],
                        start=(k == 0), stop=(k == KC - 1)),
                        c_mm if k == KC - 1 else None)
                P.wait("vector", c_mm, n["c_mm"])
                vop(lambda v, tc=tc, pr=pr, pq=pq, m=m: v.tensor_tensor(
                    out=pool2[:, B_V[m] + tc * 128:B_V[m] + (tc + 1) * 128],
                    in0=pq[:, 0:128], in1=bvb[:, pr * 128:(pr + 1) * 128],
                    op=ALU.add), c_ev)
                qgrp[0] += 1
                qev_hist.append(n["c_ev"])
            qkv_ev[pr] = n["c_ev"]

        def emit_S(m, pr, qt, segs, jlist, tp_done):
            T = STM[m]
            for j in jlist:
                off, ln = segs[j]
                span = (qt + 1) * 128
                jj = segctr[0] % 2
                segctr[0] += 1
                diag = (off + ln == span)
                if len(exp_hist) >= 2:
                    P.wait("tensor", c_exp, exp_hist[-2])
                for hh in range(2):
                    inc_here = (c_mm if (hh == 1 and not diag) else None)
                    top(lambda t, qt=qt, off=off, ln=ln, jj=jj, hh=hh, m=m,
                        diag=diag: t.matmul(
                            p[jj * 2 + hh][:, 0:ln],
                            pool2[hh * 64:(hh + 1) * 64,
                                  B_QT[m] + qt * 128:B_QT[m] + (qt + 1) * 128],
                            pool2[hh * 64:(hh + 1) * 64,
                                  B_KT[m] + off:B_KT[m] + off + ln],
                            start=True, stop=not diag,
                            tile_position=(hh * 64, 0)), inc_here)
                    if diag:
                        top(lambda t, jj=jj, hh=hh, ln=ln: t.matmul(
                            p[jj * 2 + hh][:, ln - 128:ln], iden[:], maskb[:],
                            start=False, stop=True),
                            c_mm if hh == 1 else None)
                P.wait("vector", c_mm, n["c_mm"])
                for hh in range(2):
                    vop(lambda v, jj=jj, j=j, hh=hh, ln=ln, T=T: v.reduce_max(
                        out=T["nmx"][:, hh * 8 + j:hh * 8 + j + 1],
                        in_=p[jj * 2 + hh][:, 0:ln], axis=AX.X, negate=True),
                        c_nm)
                P.wait("scalar", c_nm, n["c_nm"])
                if qt >= 1:
                    P.wait("scalar", c_tp, tp_done[qt - 1])
                for hh in range(2):
                    pb, base = Pbuf(m, hh)
                    P.op("scalar", lambda s, j=j, jj=jj, hh=hh, off=off,
                         ln=ln, base=base, pb=pb, T=T: s.activation(
                             out=pb[:, base + off:base + off + ln],
                             in_=p[jj * 2 + hh][:, 0:ln], func=AF.Exp,
                             bias=T["nmx"][:, hh * 8 + j:hh * 8 + j + 1],
                             accum_out=T["ssm"][:, hh * 8 + j:hh * 8 + j + 1]),
                         inc=[(c_exp, 1)] if hh == 1 else None)
                n["c_exp"] += 1
                exp_hist.append(n["c_exp"])

        def emit_stats(m, qt, segs, nseg):
            T = STM[m]
            nmxT = T["nmx"]; ssmT = T["ssm"]; negmT = T["negm"]
            ddvT = T["ddv"]; ccvT = T["ccv"]; Z2T = T["Z2"]; rz2T = T["rz2"]
            P.wait("vector", c_exp, n["c_exp"])
            P.wait("vector", c_nm, n["c_nm"])
            if nseg == 1:
                for hh in range(2):
                    vch(lambda v, hh=hh, ccvT=ccvT, ssmT=ssmT: v.reciprocal(
                        out=ccvT[:, hh * 8:hh * 8 + 1],
                        in_=ssmT[:, hh * 8:hh * 8 + 1]))
                P.wait("vector", c_st, n["c_st"])
            else:
                for hh in range(2):
                    P.op("vector", lambda v, hh=hh, nseg=nseg, nmxT=nmxT,
                         negmT=negmT: v.tensor_reduce(
                             out=negmT[:, hh:hh + 1],
                             in_=nmxT[:, hh * 8:hh * 8 + nseg], axis=AX.X,
                             op=ALU.min), inc=[(c_st, 1)])
                    n["c_st"] += 1
                for hh in range(2):
                    sch(lambda s, hh=hh, nseg=nseg, ccvT=ccvT, nmxT=nmxT,
                        negmT=negmT: s.activation(
                            out=ccvT[:, hh * 8:hh * 8 + nseg],
                            in_=nmxT[:, hh * 8:hh * 8 + nseg], func=AF.Exp,
                            scale=-1.0, bias=negmT[:, hh:hh + 1]))
                P.wait("vector", c_st, n["c_st"])
                for hh in range(2):
                    vop(lambda v, hh=hh, nseg=nseg, ddvT=ddvT, ssmT=ssmT,
                        ccvT=ccvT: v.tensor_tensor(
                            out=ddvT[:, hh * 8:hh * 8 + nseg],
                            in0=ssmT[:, hh * 8:hh * 8 + nseg],
                            in1=ccvT[:, hh * 8:hh * 8 + nseg], op=ALU.mult), c_st)
                P.wait("vector", c_st, n["c_st"])
                for hh in range(2):
                    P.op("vector", lambda v, hh=hh, nseg=nseg, Z2T=Z2T,
                         ddvT=ddvT: v.tensor_reduce(
                             out=Z2T[:, hh:hh + 1],
                             in_=ddvT[:, hh * 8:hh * 8 + nseg], axis=AX.X,
                             op=ALU.add), inc=[(c_st, 1)])
                    n["c_st"] += 1
                P.wait("vector", c_st, n["c_st"])
                vch(lambda v, rz2T=rz2T, Z2T=Z2T: v.reciprocal(out=rz2T, in_=Z2T))
                P.wait("vector", c_st, n["c_st"])
                for hh in range(2):
                    vop(lambda v, hh=hh, nseg=nseg, ccvT=ccvT, rz2T=rz2T:
                        v.tensor_scalar(
                            out=ccvT[:, hh * 8:hh * 8 + nseg],
                            in0=ccvT[:, hh * 8:hh * 8 + nseg],
                            scalar1=rz2T[:, hh:hh + 1], scalar2=None,
                            op0=ALU.mult), c_st)
                P.wait("vector", c_st, n["c_st"])
            for j2, (off, ln) in enumerate(segs):
                for hh in range(2):
                    pb, base = Pbuf(m, hh)
                    vop(lambda v, j2=j2, hh=hh, off=off, ln=ln, base=base,
                        pb=pb, ccvT=ccvT: v.tensor_scalar(
                            out=pb[:, base + off:base + off + ln],
                            in0=pb[:, base + off:base + off + ln],
                            scalar1=ccvT[:, hh * 8 + j2:hh * 8 + j2 + 1],
                            scalar2=None, op0=ALU.mult), c_psc)

        def emit_TAV(m, pr, qt):
            span = (qt + 1) * 128
            nch = span // 128
            p4b = pp[2][:].bitcast(BF16)
            batches = []
            for hh in range(2):
                for c0 in range(0, nch, 8):
                    batches.append((hh, list(range(c0, min(c0 + 8, nch)))))

            def emit_av(hh, cb, par, pte_need):
                P.wait("tensor", c_pte, pte_need)
                P.wait("tensor", c_oev, n["c_oev"])
                for ci, c in enumerate(cb):
                    first, last = (c == 0), (c == nch - 1)
                    blast = (ci == len(cb) - 1)
                    top(lambda t, hh=hh, c=c, ci=ci, first=first, m=m,
                        last=last, par=par: t.matmul(
                            p[6][hh * 64:(hh + 1) * 64, 0:128],
                            pool2[:, B_V[m] + c * 128 + hh * 64:
                                  B_V[m] + c * 128 + hh * 64 + 64],
                            pool2[:, C_PT + par * 1024 + ci * 128:
                                  C_PT + par * 1024 + (ci + 1) * 128],
                            start=first, stop=last,
                            tile_position=(0, hh * 64)),
                        c_avb if blast else None)

            prev_b = None
            for (hh, cb) in batches:
                pb, base = Pbuf(m, hh)
                par = tb[0] % 2
                tb[0] += 1
                P.wait("tensor", c_psc, n["c_psc"])
                P.wait("tensor", c_pte, max(0, n["c_pte"] - 1))
                for ci, c in enumerate(cb):
                    top(lambda t, c=c, ci=ci, base=base, p4b=p4b, pb=pb,
                        par=par: t.matmul(
                            p4b[:, par * 1024 + ci * 128:par * 1024 + (ci + 1) * 128],
                            pb[:, base + c * 128:base + (c + 1) * 128],
                            iden[:], start=True, stop=True,
                            is_transpose=True),
                        c_tp if ci == len(cb) - 1 else None)
                P.wait("vector", c_tp, n["c_tp"])
                P.wait("vector", c_avb, max(0, n["c_avb"] - 1))
                vop(lambda v, nb=len(cb), p4b=p4b, par=par: v.tensor_copy(
                    out=pool2[:, C_PT + par * 1024:C_PT + par * 1024 + nb * 128],
                    in_=p4b[:, par * 1024:par * 1024 + nb * 128]), c_pte)
                if prev_b is not None:
                    emit_av(*prev_b)
                prev_b = (hh, cb, par, n["c_pte"])
            if prev_b is not None:
                emit_av(*prev_b)
            sl = ost[0] % 4
            ost[0] += 1
            P.wait("vector", c_avb, n["c_avb"])
            P.wait("vector", h_o[sl], 16 * ouse[sl])
            vop(lambda v, sl=sl: v.tensor_copy(
                out=ostg[:, sl * 128:(sl + 1) * 128],
                in_=p[6][:, 0:128]), c_oev)
            P.wait("sync", c_oev, n["c_oev"])
            sdma(os_d[pr][:, qt * 128:(qt + 1) * 128],
                 ostg[:, sl * 128:(sl + 1) * 128], h_o[sl])
            ouse[sl] += 1

        for duo in range(NPAIR // 2):
            prA, prB = 2 * duo, 2 * duo + 1
            emit_qkv(0, prA)
            if prA + 2 < NPAIR:
                load_pair_w(prA + 2, qkv_ev[prA])
            emit_qkv(1, prB)
            if prB + 2 < NPAIR:
                load_pair_w(prB + 2, qkv_ev[prB])

            tpd = [[0] * QT, [0] * QT]
            acc_tp = n["c_tp"]
            for qt in range(1, QT + 1):
                for m in range(2):
                    q = qt - 1
                    if qt <= QT - 1 or True:
                        pass
            order = []
            for qt in range(1, QT):
                order.append((0, qt - 1))
                order.append((1, qt - 1))
            order.append((0, QT - 1))
            order.append((1, QT - 1))
            for (m, q) in order:
                acc_tp += 2 * ((q // 8) + 1)
                tpd[m][q] = acc_tp

            P.wait("tensor", c_ev, qkv_ev[prB])
            for qt in range(QT):
                span = (qt + 1) * 128
                nseg = (span + SEG - 1) // SEG
                segs = [(j * SEG, min(SEG, span - j * SEG)) for j in range(nseg)]
                emit_S(0, prA, qt, segs, [0], tpd[0])
                emit_S(1, prB, qt, segs, [0], tpd[1])
                if qt >= 1:
                    emit_TAV(0, prA, qt - 1)
                    emit_TAV(1, prB, qt - 1)
                for j in range(1, nseg):
                    emit_S(0, prA, qt, segs, [j], tpd[0])
                    emit_S(1, prB, qt, segs, [j], tpd[1])
                emit_stats(0, qt, segs, nseg)
                emit_stats(1, qt, segs, nseg)
            emit_TAV(0, prA, QT - 1)
            emit_TAV(1, prB, QT - 1)

        for s_o in (h_o0, h_o1, h_o2, h_o3):
            P.wait("gpsimd", s_o, n[s_o.name])
        P.wait("gpsimd", c_avb, n["c_avb"])
        for pr in range(NPAIR):
            gdma(pool2[:, pr * 1024:(pr + 1) * 1024], wo_d[pr], g_wo)
        ATT_W = n["g_wo"]
        yb = pool3[:, 4096:8192]
        ysq = pool3[:, 8192:12288]
        xt2 = ft[:, FT_X:FT_X + 4096]
        P.wait("tensor", c_exp, n["c_exp"])
        P.wait("tensor", c_oev, n["c_oev"])
        P.wait("tensor", c_pte, n["c_pte"])
        P.wait("tensor", c_ev, n["c_ev"])
        for ts in range(NT):
            P.wait("gpsimd", c_ev, n["c_ev"])
            P.wait("gpsimd", c_st, n["c_st"])
            P.wait("gpsimd", h_xw, n["h_xw"])
            for pr in range(NPAIR):
                gdma(pool3[:, pr * 512:(pr + 1) * 512],
                     os_d[pr][:, ts * 512:(ts + 1) * 512], g_or)
            gdma(xt2.rearrange("p (k t) -> p k t", k=KC),
                 out_v[:, :, ts * 512:(ts + 1) * 512], g_xb)
            P.wait("tensor", g_wo, ATT_W)
            P.wait("tensor", g_or, n["g_or"])
            P.wait("tensor", c_st, n["c_st"])
            P.wait("vector", c_st, n["c_st"])
            for d in range(KC):
                P.wait("tensor", c_ev, max(0, n["c_ev"] - 1))
                for pr in range(NPAIR):
                    top(lambda t, d=d, pr=pr: t.matmul(
                        p[d % 2][:, :],
                        pool2[:, pr * 1024 + d * 128:pr * 1024 + (d + 1) * 128],
                        pool3[:, pr * 512:(pr + 1) * 512],
                        start=(pr == 0), stop=(pr == NPAIR - 1)),
                        c_mm if pr == NPAIR - 1 else None)
                P.wait("vector", c_mm, n["c_mm"])
                vop(lambda v, d=d: v.tensor_scalar(
                    out=yb[:, d * 512:(d + 1) * 512], in0=p[d % 2][:, :],
                    scalar1=bos[:, d:d + 1], scalar2=None, op0=ALU.add), c_ev)
            P.wait("scalar", c_ev, n["c_ev"])
            P.wait("scalar", c_mm, n["c_mm"])
            P.op("scalar", lambda s: s.activation(out=ysq, in_=yb, func=AF.Square),
                 inc=[(c_ln, 1)]); n["c_ln"] += 1
            P.wait("tensor", c_ln, n["c_ln"])
            P.wait("tensor", c_st, n["c_st"])
            stats_mms(yb, ysq)
            P.wait("vector", c_mm, n["c_mm"])
            r_mu, r_rs = ln_stats_and_rstd()
            P.wait("tensor", c_st, n["c_st"])
            bcast_mms(r_mu, r_rs)
            P.wait("vector", c_mm, n["c_mm"])
            P.wait("vector", g_xb, n["g_xb"])
            for k in range(KC):
                stg = ft[:, 3584:3584 + 512]
                vch(lambda v, k=k, stg=stg: v.tensor_tensor(
                    out=stg, in0=yb[:, k * 512:(k + 1) * 512], in1=p[5][:, :],
                    op=ALU.subtract))
                vch(lambda v, stg=stg: v.tensor_tensor(
                    out=stg, in0=stg, in1=p[6][:, :], op=ALU.mult))
                sch(lambda s, k=k, stg=stg: s.activation(
                    out=stg, in_=stg, func=AF.Identity,
                    scale=gams[:, k:k + 1], bias=bets[:, k:k + 1]))
                vch(lambda v, k=k, stg=stg: v.tensor_tensor(
                    out=xt2[:, k * 512:(k + 1) * 512],
                    in0=xt2[:, k * 512:(k + 1) * 512], in1=stg, op=ALU.add))
            P.wait("sync", c_st, n["c_st"])
            sdma(out_v[:, :, ts * 512:(ts + 1) * 512],
                 xt2.rearrange("p (k t) -> p k t", k=KC), h_xw)

    ffn_call(0, h_pre, 16, 0)
    XW0 = n["h_xw"]
    attention()
    ffn_call(1, h_xw, XW0, 16)
    P.wait("gpsimd", g_acca, n["g_acca"])
    P.wait("gpsimd", g_accb, n["g_accb"])
    P.wait("sync", h_xw, n["h_xw"])
    P.wait("sync", h_pre, PRE_H)
    P.wait("gpsimd", g_pre, PRE_G)

    P.check()

    blk = st.enter_context(nc.Block())

    @blk.gpsimd
    def _(g):
        P.emit("gpsimd", g)

    @blk.tensor
    def _(t):
        P.emit("tensor", t)

    @blk.scalar
    def _(s):
        P.emit("scalar", s)

    @blk.vector
    def _(v):
        P.emit("vector", v)

    @blk.sync
    def _(sy):
        P.emit("sync", sy)

    st.close()
    return nc


def _layouts(inputs):
    bf = ml_dtypes.bfloat16
    w1 = inputs["L0_ffn_w1"]
    w2 = inputs["L0_ffn_w2"]
    w1r = np.ascontiguousarray(
        w1.reshape(E, KC, 128, MC, 128).transpose(0, 2, 3, 1, 4))
    w2r = np.ascontiguousarray(w2.reshape(E, MC, 128, D).transpose(0, 2, 1, 3))
    b1r = np.ascontiguousarray(
        inputs["L0_ffn_b1"].reshape(E, MC, 128).transpose(2, 0, 1).reshape(128, E * MC))
    b2r = np.ascontiguousarray(inputs["L0_ffn_b2"])
    gwT = np.ascontiguousarray(
        inputs["L0_ffn_gate_w"].T.reshape(KC, 128, 8))
    gbb = np.ascontiguousarray(np.broadcast_to(inputs["L0_ffn_gate_b"], (128, 8)))

    def qk_r(w):
        return np.ascontiguousarray(
            w.reshape(KC, 128, NPAIR, 128).transpose(2, 1, 0, 3))

    common = dict(
        w1r=w1r, w2r=w2r, b1r=b1r, b2r=b2r, gwT=gwT, gbb=gbb,
        wqr=qk_r(inputs["L0_attn_Wq"]),
        wkr=qk_r(inputs["L0_attn_Wk"]),
        wvr=qk_r(inputs["L0_attn_Wv"]),
        wor=np.ascontiguousarray(inputs["L0_attn_Wo"].reshape(NPAIR, 128, D)),
        bqr=np.ascontiguousarray(inputs["L0_attn_bq"].reshape(NPAIR, 128).T),
        bkr=np.ascontiguousarray(inputs["L0_attn_bk"].reshape(NPAIR, 128).T),
        bvbr=np.ascontiguousarray(np.broadcast_to(
            inputs["L0_attn_bv"].reshape(1, NPAIR * 128),
            (128, NPAIR * 128))).astype(bf),
        bor=np.ascontiguousarray(inputs["L0_attn_bo"].reshape(KC, 128).T),
        gamr=np.ascontiguousarray(inputs["L0_attn_gamma"].reshape(KC, 128).T),
        betr=np.ascontiguousarray(inputs["L0_attn_beta"].reshape(KC, 128).T),
        idenb=np.eye(128, dtype=np.float32).astype(bf),
        maskb=np.where(np.tril(np.ones((128, 128), dtype=bool)), 0.0,
                       -1e30).astype(np.float32).astype(bf),
        onesb=np.ones((128, 1), dtype=np.float32).astype(bf),
        onesf=np.ones((1, 128), dtype=np.float32),
    )
    eselb = np.zeros((8, E * 128), dtype=np.float32)
    for e in range(E):
        eselb[e, e * 128:(e + 1) * 128] = 1.0
    common["eselb"] = eselb.astype(bf)
    return common


def kernel(**inputs):
    x = inputs["L0_attn_x"]
    seq = x.shape[1]
    common = _layouts(inputs)
    nc = build_program(seq)
    in_maps = []
    for c in range(x.shape[0]):
        m = dict(common)
        m["x"] = np.ascontiguousarray(x[c].T)
        in_maps.append(m)
    res = run_bass_kernel_spmd(nc, in_maps, core_ids=list(range(x.shape[0])))
    outs = [np.asarray(r["out"]).T for r in res.results]
    return np.stack(outs, axis=0).astype(np.float32)


# version 13
# speedup vs baseline: 1.1395x; 1.1395x over previous
import contextlib
import numpy as np
import ml_dtypes
import concourse.bass as bass
import concourse.mybir as mybir
from concourse.bass_utils import run_bass_kernel_spmd

F32 = mybir.dt.float32
BF16 = mybir.dt.bfloat16
AF = mybir.ActivationFunctionType
ALU = mybir.AluOpType
AX = mybir.AxisListType

D = 1024
HID = 4096
E = 8
KC = D // 128
MC = HID // 128
NCORES = 8
NPAIR = 8
EPS = 1e-5
SEG = 512


class Prog:
    ENGS = ("tensor", "scalar", "vector", "gpsimd", "sync")

    def __init__(self):
        self.streams = {e: [] for e in self.ENGS}

    def wait(self, eng, sem, count):
        if count > 0:
            self.streams[eng].append(("w", sem, count))

    def op(self, eng, fn, inc=None):
        self.streams[eng].append(("o", fn, inc or []))

    def check(self):
        sems, pos = {}, {e: 0 for e in self.ENGS}
        progress = True
        while progress:
            progress = False
            for e in self.ENGS:
                while pos[e] < len(self.streams[e]):
                    it = self.streams[e][pos[e]]
                    if it[0] == "w":
                        if sems.get(it[1].name, 0) < it[2]:
                            break
                    else:
                        for sem, k in it[2]:
                            sems[sem.name] = sems.get(sem.name, 0) + k
                    pos[e] += 1
                    progress = True
        stuck = [(e, pos[e], len(self.streams[e])) for e in self.ENGS
                 if pos[e] < len(self.streams[e])]
        if stuck:
            parts = []
            for e, pp, nn_ in stuck:
                it = self.streams[e][pp]
                parts.append(f"{e}@{pp}/{nn_} waits {it[1].name}>={it[2]} "
                             f"(have {sems.get(it[1].name, 0)})")
            raise RuntimeError("Prog deadlock: " + "; ".join(parts))

    def emit(self, eng_name, eng):
        for it in self.streams[eng_name]:
            if it[0] == "w":
                eng.wait_ge(it[1], it[2])
            else:
                ins = it[1](eng)
                for sem, k in it[2]:
                    ins.then_inc(sem, k)


def build_program(seq):
    NT = seq // 512
    QT = seq // 128
    TC = seq // 128

    nc = bass.Bass("TRN2", target_bir_lowering=False)
    dr = lambda name, s, d, k: nc.dram_tensor(name, s, d, kind=k).ap()

    x_d = dr("x", [D, seq], F32, "ExternalInput")
    out_d = dr("out", [D, seq], F32, "ExternalOutput")
    w1_d = dr("w1r", [E, 128, MC, KC, 128], F32, "ExternalInput")
    w2_d = dr("w2r", [E, 128, MC, D], F32, "ExternalInput")
    b1_d = dr("b1r", [128, E * MC], F32, "ExternalInput")
    b2_d = dr("b2r", [8, D], F32, "ExternalInput")
    gwT_d = dr("gwT", [KC, 128, 8], F32, "ExternalInput")
    gbb_d = dr("gbb", [128, 8], F32, "ExternalInput")
    wq_d = dr("wqr", [NPAIR, 128, KC, 128], F32, "ExternalInput")
    wk_d = dr("wkr", [NPAIR, 128, KC, 128], F32, "ExternalInput")
    wv_d = dr("wvr", [NPAIR, 128, KC, 128], F32, "ExternalInput")
    wo_d = dr("wor", [NPAIR, 128, D], F32, "ExternalInput")
    bq_d = dr("bqr", [128, NPAIR], F32, "ExternalInput")
    bk_d = dr("bkr", [128, NPAIR], F32, "ExternalInput")
    bvb_d = dr("bvbr", [128, NPAIR * 128], BF16, "ExternalInput")
    bo_d = dr("bor", [128, KC], F32, "ExternalInput")
    gam_d = dr("gamr", [128, KC], F32, "ExternalInput")
    bet_d = dr("betr", [128, KC], F32, "ExternalInput")
    iden_d = dr("idenb", [128, 128], BF16, "ExternalInput")
    mask_d = dr("maskb", [128, 128], BF16, "ExternalInput")
    ones_d = dr("onesb", [128, 1], BF16, "ExternalInput")
    onesf_d = dr("onesf", [1, 128], F32, "ExternalInput")
    esel_d = dr("eselb", [8, E * 128], BF16, "ExternalInput")
    xlns_d = dr("xlns", [D, seq], BF16, "Internal")
    os_d = dr("osp", [NPAIR, 128, seq], BF16, "Internal")

    st = contextlib.ExitStack()
    sb = lambda name, s, d: st.enter_context(nc.sbuf_tensor(name, s, d))

    pool1 = sb("pool1", [128, 32768], BF16)
    pool2 = sb("pool2", [128, 32768], BF16)
    pool3 = sb("pool3", [128, 16384], BF16)
    ft = sb("ft", [128, 8192], F32)
    w8 = sb("w8", [8, seq], BF16)
    b1t = sb("b1t", [128, E * MC], F32)
    b2t = sb("b2t", [8, D], BF16)
    gwT = sb("gwTs", [128, KC * 8], BF16)
    gbb = sb("gbbs", [128, 8], F32)
    bqs = sb("bqs", [128, NPAIR], F32)
    bks = sb("bks", [128, NPAIR], F32)
    bvb = sb("bvbs", [128, NPAIR * 128], BF16)
    bos = sb("boss", [128, KC], F32)
    gams = sb("gams", [128, KC], F32)
    bets = sb("betss", [128, KC], F32)
    iden = sb("idens", [128, 128], BF16)
    maskb = sb("maskbs", [128, 128], BF16)
    onesb = sb("onesbs", [128, 1], BF16)
    onesf = sb("onesfs", [1, 128], F32)
    stat = sb("stat", [128, 32], F32)
    esel = ft[:].bitcast(BF16)[0:8, 6144:6144 + E * 128]

    FT_STG = (0, 512)
    FT_WB = (1024, 1536)
    FT_ROW = 2048
    FT_STX = 3584 + 512
    FT_STX = 3840
    FT_X = 4096
    FT_Y = 0
    rowsB = ft
    stx = ft

    pp = [st.enter_context(nc.psum_tensor(f"pp{i}", [128, 1024], F32))
          for i in range(4)]
    p = [pp[i // 2][:, (i % 2) * 512:(i % 2 + 1) * 512] for i in range(8)]

    sem = lambda name: st.enter_context(nc.semaphore(name))
    g_w1 = sem("g_w1"); g_w2 = sem("g_w2"); g_xb = sem("g_xb")
    g_acca = sem("g_acca"); g_accb = sem("g_accb")
    g_att0 = sem("g_att0"); g_att1 = sem("g_att1"); g_wo = sem("g_wo"); g_or = sem("g_or")
    g_pre = sem("g_pre")
    h_xlw = sem("h_xlw"); h_xlr0 = sem("h_xlr0"); h_xlr1 = sem("h_xlr1"); h_ow = sem("h_ow")
    h_o0 = sem("h_o0"); h_o1 = sem("h_o1"); h_o2 = sem("h_o2"); h_o3 = sem("h_o3")
    h_xw = sem("h_xw"); h_pre = sem("h_pre")
    c_h = sem("c_h"); c_gel = sem("c_gel"); c_o = sem("c_o"); c_ev = sem("c_ev")
    c_wb = sem("c_wb"); c_wbe = sem("c_wbe"); c_ln = sem("c_ln")
    c_st = sem("c_st"); c_g = sem("c_g"); c_mm = sem("c_mm")
    c_nm = sem("c_nm"); c_exp = sem("c_exp"); c_psc = sem("c_psc")
    c_tp = sem("c_tp"); c_pte = sem("c_pte"); c_avb = sem("c_avb")
    c_oev = sem("c_oev")

    P = Prog()
    nstage = [0]
    n = {}
    for s_ in (g_w1, g_w2, g_xb, g_acca, g_accb, g_att0, g_att1, g_wo, g_or,
               g_pre, h_xlw, h_xlr0, h_xlr1, h_ow, h_o0, h_o1, h_o2, h_o3,
               h_xw, h_pre, c_h, c_gel,
               c_o, c_ev, c_wb, c_wbe, c_ln, c_st, c_g, c_mm, c_nm, c_exp,
               c_psc, c_tp, c_pte, c_avb, c_oev):
        n[s_.name] = 0

    def gdma(dst, src, s, accum=False):
        if accum:
            P.op("gpsimd", lambda g, dst=dst, src=src: g.dma_start(
                out=dst, in_=src, accum_op=ALU.add), inc=[(s, 16)])
        else:
            P.op("gpsimd", lambda g, dst=dst, src=src: g.dma_start(
                out=dst, in_=src), inc=[(s, 16)])
        n[s.name] += 16

    def sdma(dst, src, s):
        P.op("sync", lambda sy, dst=dst, src=src: sy.dma_start(out=dst, in_=src),
             inc=[(s, 16)])
        n[s.name] += 16

    def top(fn, s=None):
        inc = [(s, 1)] if s is not None else []
        P.op("tensor", fn, inc=inc)
        if s is not None:
            n[s.name] += 1

    def vop(fn, s=None):
        inc = [(s, 1)] if s is not None else []
        P.op("vector", fn, inc=inc)
        if s is not None:
            n[s.name] += 1

    def vch(fn):
        P.wait("vector", c_st, n["c_st"])
        P.op("vector", fn, inc=[(c_st, 1)])
        n["c_st"] += 1

    def sch(fn):
        P.wait("scalar", c_st, n["c_st"])
        P.op("scalar", fn, inc=[(c_st, 1)])
        n["c_st"] += 1

    out_v = out_d.rearrange("(k p) t -> p k t", p=128)
    xlns_v = xlns_d.rearrange("(k p) t -> p k t", p=128)

    for dst, src in ((b1t, b1_d), (gbb, gbb_d), (bqs, bq_d), (bks, bk_d),
                     (bvb, bvb_d), (bos, bo_d), (gams, gam_d), (bets, bet_d),
                     (onesf, onesf_d), (b2t, b2_d), (iden, iden_d),
                     (maskb, mask_d), (onesb, ones_d), (esel, esel_d)):
        gdma(dst[:], src[:], g_pre)
    gdma(gwT[:].rearrange("p (k j) -> p k j", k=KC),
         gwT_d.rearrange("k i j -> i k j"), g_pre)
    sdma(out_d[:], x_d[:], h_pre)
    PRE_G = n["g_pre"]
    PRE_H = n["h_pre"]
    P.wait("tensor", g_pre, PRE_G)
    P.wait("scalar", g_pre, PRE_G)
    P.wait("vector", g_pre, PRE_G)

    def ln_stats_and_rstd():
        r_mu = rowsB[0:1, 1536:2048]
        r_rs = rowsB[0:1, 2048:2560]
        r_t1 = rowsB[0:1, 3584:4096]
        vch(lambda v: v.tensor_scalar(out=r_mu, in0=p[5][0:1, :],
                                      scalar1=1.0 / D, scalar2=None, op0=ALU.mult))
        vch(lambda v: v.tensor_scalar(out=r_rs, in0=p[6][0:1, :],
                                      scalar1=1.0 / D, scalar2=None, op0=ALU.mult))
        vch(lambda v: v.tensor_tensor(out=r_t1, in0=r_mu, in1=r_mu, op=ALU.mult))
        vch(lambda v: v.tensor_tensor(out=r_rs, in0=r_rs, in1=r_t1, op=ALU.subtract))
        vch(lambda v: v.tensor_scalar(out=r_rs, in0=r_rs, scalar1=EPS,
                                      scalar2=None, op0=ALU.add))
        sch(lambda s: s.activation(out=r_rs, in_=r_rs, func=AF.Sqrt))
        vch(lambda v: v.reciprocal(out=r_rs, in_=r_rs))
        return r_mu, r_rs

    def stats_mms(src_a, src_b):
        for k in range(KC):
            top(lambda t, k=k: t.matmul(
                p[5][0:1, :], onesb[:], src_a[:, k * 512:(k + 1) * 512],
                start=(k == 0), stop=(k == KC - 1)))
            top(lambda t, k=k: t.matmul(
                p[6][0:1, :], onesb[:], src_b[:, k * 512:(k + 1) * 512],
                start=(k == 0), stop=(k == KC - 1)),
                c_mm if k == KC - 1 else None)

    def bcast_mms(r_mu, r_rs):
        top(lambda t: t.matmul(p[5][:, :], onesf[:], r_mu, start=True, stop=True))
        top(lambda t: t.matmul(p[6][:, :], onesf[:], r_rs, start=True, stop=True),
            c_mm)

    def ffn_call(call, xdep_sem, xdep_base, xdep_step):
        xb = pool3[:, 0:4096]
        xsq = pool3[:, 4096:8192]
        xlt = pool3[:, 8192:12288]
        gtile = pool3[:, 12288:12296]
        ev_hist = []

        for ts in range(NT):
            P.wait("gpsimd", c_st, n["c_st"])
            P.wait("gpsimd", xdep_sem, xdep_base + xdep_step * (ts + 1))
            gdma(xb.rearrange("p (k t) -> p k t", k=KC),
                 out_v[:, :, ts * 512:(ts + 1) * 512], g_xb)
            P.wait("scalar", g_xb, n["g_xb"])
            P.wait("scalar", c_mm, n["c_mm"])
            P.op("scalar", lambda s: s.activation(out=xsq, in_=xb, func=AF.Square),
                 inc=[(c_ln, 1)]); n["c_ln"] += 1
            P.wait("tensor", c_ln, n["c_ln"])
            P.wait("tensor", c_st, n["c_st"])
            stats_mms(xb, xsq)
            P.wait("vector", c_mm, n["c_mm"])
            r_mu, r_rs = ln_stats_and_rstd()
            P.wait("tensor", c_st, n["c_st"])
            bcast_mms(r_mu, r_rs)
            P.wait("vector", c_mm, n["c_mm"])
            P.wait("vector", c_g, n["c_g"])
            P.wait("vector", h_xlw, n["h_xlw"])
            for k in range(KC):
                stg = ft[:, 3584:3584 + 512]
                vch(lambda v, k=k, stg=stg: v.tensor_tensor(
                    out=stg, in0=xb[:, k * 512:(k + 1) * 512], in1=p[5][:, :],
                    op=ALU.subtract))
                vch(lambda v, k=k, stg=stg: v.tensor_tensor(
                    out=xlt[:, k * 512:(k + 1) * 512], in0=stg, in1=p[6][:, :],
                    op=ALU.mult))
            P.wait("sync", c_st, n["c_st"])
            P.wait("sync", h_xlr0, n["h_xlr0"])
            P.wait("sync", h_xlr1, n["h_xlr1"])
            sdma(xlns_v[:, :, ts * 512:(ts + 1) * 512],
                 xlt.rearrange("p (k t) -> p k t", k=KC), h_xlw)
            for g4 in range(4):
                tok0 = g4 * 128
                P.wait("tensor", c_g, n["c_g"])
                P.wait("tensor", c_pte, n["c_pte"])
                P.wait("tensor", c_wbe, n["c_wbe"])
                P.wait("tensor", c_st, n["c_st"])
                for k in range(KC):
                    top(lambda t, k=k, tok0=tok0: t.matmul(
                        p[4][:, 0:8],
                        xlt[:, k * 512 + tok0:k * 512 + tok0 + 128],
                        gwT[:, k * 8:(k + 1) * 8],
                        start=(k == 0), stop=(k == KC - 1)),
                        c_mm if k == KC - 1 else None)
                gg = stat[:, 0:8]; m1 = stat[:, 8:9]; m2 = stat[:, 9:10]
                msk1 = stat[:, 10:18]; tmp8 = stat[:, 18:26]
                e21 = stat[:, 26:27]; wA = stat[:, 27:28]; wB = stat[:, 28:29]
                P.wait("vector", c_mm, n["c_mm"])
                vch(lambda v: v.tensor_tensor(out=gg, in0=p[4][:, 0:8],
                                              in1=gbb[:], op=ALU.add))
                vch(lambda v: v.reduce_max(out=m1, in_=gg, axis=AX.X))
                vch(lambda v: v.tensor_scalar(out=msk1, in0=gg, scalar1=m1,
                                              scalar2=None, op0=ALU.is_ge))
                vch(lambda v: v.tensor_scalar(out=tmp8, in0=msk1, scalar1=1e30,
                                              scalar2=None, op0=ALU.mult))
                vch(lambda v: v.tensor_tensor(out=tmp8, in0=gg, in1=tmp8,
                                              op=ALU.subtract))
                vch(lambda v: v.reduce_max(out=m2, in_=tmp8, axis=AX.X))
                vch(lambda v: v.tensor_tensor(out=e21, in0=m2, in1=m1,
                                              op=ALU.subtract))
                sch(lambda s: s.activation(out=e21, in_=e21, func=AF.Exp))
                vch(lambda v: v.tensor_scalar(out=wA, in0=e21, scalar1=1.0,
                                              scalar2=None, op0=ALU.add))
                vch(lambda v: v.reciprocal(out=wA, in_=wA))
                vch(lambda v: v.tensor_scalar(out=wB, in0=wA, scalar1=-1.0,
                                              scalar2=1.0, op0=ALU.mult,
                                              op1=ALU.add))
                vch(lambda v: v.tensor_scalar(out=tmp8, in0=tmp8, scalar1=m2,
                                              scalar2=None, op0=ALU.is_ge))
                vch(lambda v: v.tensor_scalar(out=tmp8, in0=tmp8, scalar1=wB,
                                              scalar2=None, op0=ALU.mult))
                vch(lambda v: v.tensor_scalar(out=msk1, in0=msk1, scalar1=wA,
                                              scalar2=None, op0=ALU.mult))
                vch(lambda v: v.tensor_tensor(out=msk1, in0=msk1, in1=tmp8,
                                              op=ALU.add))
                vch(lambda v: v.tensor_scalar(out=gtile, in0=msk1, scalar1=0.5,
                                              scalar2=None, op0=ALU.mult))
                p4b = p[4][:].bitcast(BF16)
                P.wait("tensor", c_st, n["c_st"])
                top(lambda t, p4b=p4b: t.matmul(
                    p4b[0:8, 128:256], gtile, iden[:], start=True, stop=True,
                    is_transpose=True), c_mm)
                P.wait("vector", c_mm, n["c_mm"])
                vop(lambda v, ts=ts, g4=g4, p4b=p4b: v.tensor_copy(
                    out=w8[0:8, ts * 512 + g4 * 128:ts * 512 + (g4 + 1) * 128],
                    in_=p4b[0:8, 128:256]), c_g)
            for d in range(KC):
                pb = p[d % 2]
                P.wait("tensor", c_ev, max(0, n["c_ev"] - 1))
                P.wait("tensor", c_g, n["c_g"])
                top(lambda t, d=d, ts=ts, pb=pb: t.matmul(
                    pb[:, :], b2t[0:8, d * 128:(d + 1) * 128],
                    w8[0:8, ts * 512:(ts + 1) * 512], start=True, stop=True),
                    c_mm)
                sg = nstage[0] % 2
                nstage[0] += 1
                sacc = g_acca if sg == 0 else g_accb
                stg = ft[:, FT_STG[sg]:FT_STG[sg] + 512]
                P.wait("vector", c_mm, n["c_mm"])
                P.wait("vector", sacc, n[sacc.name])
                vop(lambda v, pb=pb, stg=stg: v.tensor_copy(out=stg, in_=pb[:, :]),
                    c_ev)
                P.wait("gpsimd", c_ev, n["c_ev"])
                gdma(out_d.rearrange("(c p) t -> c p t", p=128)[d, :, ts * 512:(ts + 1) * 512],
                     stg, sacc, accum=True)

        LN_ACCA = n["g_acca"]
        LN_ACCB = n["g_accb"]
        XLW_ALL = n["h_xlw"]
        W8_DONE = n["c_g"]

        xl2 = ft[:, FT_X:FT_X + 4096].bitcast(BF16)
        h_base = n["c_h"]; o_base = n["c_o"]; gel_base = n["c_gel"]
        gw1_base = n["g_w1"]; gw2_base = n["g_w2"]
        xlr_bases = (n["h_xlr0"], n["h_xlr1"])

        def load_w1(e):
            for pc in range(4):
                if e > 0:
                    P.wait("gpsimd", c_h,
                           h_base + ((e - 1) * NT + NT - 1) * MC + 8 * (pc + 1))
                P.wait("gpsimd", g_w1, gw1_base + 16 * (4 * e + pc))
                gdma(pool1[:, pc * 8192:(pc + 1) * 8192].rearrange(
                        "p (m k j) -> p m k j", m=8, k=KC),
                     w1_d[e, :, pc * 8:(pc + 1) * 8], g_w1)

        def load_w2(e):
            if e > 0:
                P.wait("gpsimd", c_o, o_base + e * NT * KC)
            gdma(pool2[:, :].rearrange("p (m dd) -> p m dd", m=MC),
                 w2_d[e], g_w2)

        P.wait("gpsimd", c_ev, n["c_ev"])
        load_w1(0)
        load_w2(0)

        P.wait("tensor", c_ev, n["c_ev"])
        for e in range(E):
            for ts in range(NT):
                idx = e * NT + ts
                sxlr = h_xlr0 if idx % 2 == 0 else h_xlr1
                if idx >= 2:
                    P.wait("sync", c_h, h_base + (idx - 1) * MC)
                P.wait("sync", h_xlw, XLW_ALL)
                sdma(xl2[:, (idx % 2) * 4096:(idx % 2 + 1) * 4096].rearrange(
                        "p (k t) -> p k t", k=KC),
                     xlns_v[:, :, ts * 512:(ts + 1) * 512], sxlr)
                xt = xl2[:, (idx % 2) * 4096:(idx % 2 + 1) * 4096]

                wbslot = idx % 2
                wbv = ft[:, FT_WB[wbslot]:FT_WB[wbslot] + 512]
                P.wait("tensor", c_wbe, max(0, n["c_wbe"] - 1))
                P.wait("tensor", c_g, W8_DONE)
                P.wait("tensor", c_pte, n["c_pte"])
                top(lambda t, e=e, ts=ts: t.matmul(
                    p[4][:, :], esel[0:8, e * 128:(e + 1) * 128],
                    w8[0:8, ts * 512:(ts + 1) * 512], start=True, stop=True),
                    c_wb)
                P.wait("vector", c_wb, n["c_wb"])
                if idx >= 2:
                    P.wait("vector", c_ev, ev_hist[idx - 2])
                vop(lambda v, wbv=wbv: v.tensor_copy(out=wbv, in_=p[4][:, :]),
                    c_wbe)

                sxlr = h_xlr0 if idx % 2 == 0 else h_xlr1
                P.wait("tensor", sxlr, xlr_bases[idx % 2] + 16 * (idx // 2 + 1))
                for m in range(MC):
                    gi = idx * MC + m
                    P.wait("tensor", g_w1, gw1_base + 16 * (4 * e + m // 8 + 1))
                    if gi >= 2:
                        P.wait("tensor", c_gel, gel_base + gi - 1)
                    for k in range(KC):
                        top(lambda t, m=m, k=k, xt=xt: t.matmul(
                            p[m % 2][:, :],
                            pool1[:, m * 1024 + k * 128:m * 1024 + (k + 1) * 128],
                            xt[:, k * 512:(k + 1) * 512],
                            start=(k == 0), stop=(k == KC - 1)),
                            c_h if k == KC - 1 else None)
                    P.wait("scalar", c_h, n["c_h"])
                    if idx > 0:
                        P.wait("scalar", c_o, o_base + idx * KC)
                    P.op("scalar", lambda s, m=m, e=e: s.activation(
                        out=pool3[:, m * 512:(m + 1) * 512], in_=p[m % 2][:, :],
                        func=AF.Gelu, bias=b1t[:, e * MC + m:e * MC + m + 1]),
                        inc=[(c_gel, 1)]); n["c_gel"] += 1
                P.wait("tensor", c_gel, gel_base + (idx + 1) * MC)
                P.wait("tensor", g_w2, gw2_base + 16 * (e + 1))
                P.wait("tensor", c_wbe, n["c_wbe"])
                for d in range(KC):
                    P.wait("tensor", c_ev, max(0, n["c_ev"] - 1))
                    for m in range(MC):
                        top(lambda t, d=d, m=m: t.matmul(
                            p[2 + d % 2][:, :],
                            pool2[:, m * 1024 + d * 128:m * 1024 + (d + 1) * 128],
                            pool3[:, m * 512:(m + 1) * 512],
                            start=(m == 0), stop=(m == MC - 1)),
                            c_o if m == MC - 1 else None)
                    sg = nstage[0] % 2
                    nstage[0] += 1
                    sacc = g_acca if sg == 0 else g_accb
                    stg = ft[:, FT_STG[sg]:FT_STG[sg] + 512]
                    P.wait("vector", c_o, n["c_o"])
                    P.wait("vector", sacc, n[sacc.name])
                    vop(lambda v, d=d, stg=stg, wbv=wbv: v.tensor_tensor(
                        out=stg, in0=p[2 + d % 2][:, :], in1=wbv, op=ALU.mult),
                        c_ev)
                    P.wait("gpsimd", c_ev, n["c_ev"])
                    if e == 0 and ts == 0 and d == 0:
                        P.wait("gpsimd", g_acca, LN_ACCA)
                        P.wait("gpsimd", g_accb, LN_ACCB)
                    gdma(out_d.rearrange("(c p) t -> c p t", p=128)[d, :, ts * 512:(ts + 1) * 512],
                         stg, sacc, accum=True)
                ev_hist.append(n["c_ev"])
            if e + 1 < E:
                load_w1(e + 1)
                load_w2(e + 1)

    def attention():
        ACC1A = n["g_acca"]
        ACC1B = n["g_accb"]
        xb = pool3[:, 0:4096]
        xsq = pool3[:, 4096:8192]
        for ts in range(NT):
            P.wait("gpsimd", c_st, n["c_st"])
            P.wait("gpsimd", g_acca, ACC1A)
            P.wait("gpsimd", g_accb, ACC1B)
            gdma(xb.rearrange("p (k t) -> p k t", k=KC),
                 out_v[:, :, ts * 512:(ts + 1) * 512], g_xb)
            P.wait("scalar", g_xb, n["g_xb"])
            P.wait("scalar", c_mm, n["c_mm"])
            P.op("scalar", lambda s: s.activation(out=xsq, in_=xb, func=AF.Square),
                 inc=[(c_ln, 1)]); n["c_ln"] += 1
            P.wait("tensor", c_ln, n["c_ln"])
            P.wait("tensor", c_st, n["c_st"])
            stats_mms(xb, xsq)
            P.wait("vector", c_mm, n["c_mm"])
            r_mu, r_rs = ln_stats_and_rstd()
            P.wait("tensor", c_st, n["c_st"])
            bcast_mms(r_mu, r_rs)
            P.wait("vector", c_mm, n["c_mm"])
            for k in range(KC):
                stg = ft[:, 3584:3584 + 512]
                vch(lambda v, k=k, stg=stg: v.tensor_tensor(
                    out=stg, in0=xb[:, k * 512:(k + 1) * 512], in1=p[5][:, :],
                    op=ALU.subtract))
                vch(lambda v, k=k, ts=ts, stg=stg: v.tensor_tensor(
                    out=pool1[:, k * seq + ts * 512:k * seq + (ts + 1) * 512],
                    in0=stg, in1=p[6][:, :], op=ALU.mult))
        XLN_DONE = n["c_st"]

        S_ = seq
        B_QT = (0, 12288)
        B_KT = (4096, 16384)
        B_V = (8192, 20480)
        C_PT = 24576
        WSLOT = 26624
        ostg = ft[:].bitcast(BF16)[:, 5400:5912]
        segctr = [0]
        exp_hist = []

        def Pbuf(m, hh):
            return pool3, m * 8192 + hh * 4096

        def stats_tiles(m):
            b = 2560 + m * 70
            return dict(nmx=ft[:, b:b + 16], ssm=ft[:, b + 16:b + 32],
                        negm=ft[:, b + 32:b + 34], ddv=ft[:, b + 34:b + 50],
                        ccv=ft[:, b + 50:b + 66], Z2=ft[:, b + 66:b + 68],
                        rz2=ft[:, b + 68:b + 70])
        STM = (stats_tiles(0), stats_tiles(1))

        def load_pair_w(pr, gate_ev):
            slot = pr % 2
            satt = g_att0 if slot == 0 else g_att1
            P.wait("gpsimd", c_ev, gate_ev)
            for wi, wd in ((0, wq_d), (1, wk_d), (2, wv_d)):
                gdma(pool2[:, WSLOT + (slot * 3 + wi) * 1024:
                           WSLOT + (slot * 3 + wi + 1) * 1024].rearrange(
                               "p (k j) -> p k j", k=KC), wd[pr], satt)

        load_pair_w(0, n["c_ev"])
        load_pair_w(1, n["c_ev"])
        qkv_ev = {}
        qgrp = [0]
        qev_hist = []
        ost = [0]
        tb = [0]
        h_o = (h_o0, h_o1, h_o2, h_o3)
        ouse = [0, 0, 0, 0]

        P.wait("tensor", c_wbe, n["c_wbe"])

        def emit_qkv(m, pr):
            P.wait("tensor", c_exp, n["c_exp"])
            slot = pr % 2
            wq_s = pool2[:, WSLOT + (slot * 3 + 0) * 1024:WSLOT + (slot * 3 + 1) * 1024]
            wk_s = pool2[:, WSLOT + (slot * 3 + 1) * 1024:WSLOT + (slot * 3 + 2) * 1024]
            wv_s = pool2[:, WSLOT + (slot * 3 + 2) * 1024:WSLOT + (slot * 3 + 3) * 1024]
            satt = g_att0 if slot == 0 else g_att1
            P.wait("tensor", satt, 16 * 3 * (pr // 2 + 1))
            P.wait("tensor", c_st, XLN_DONE)
            for which, wsb, bias in (("q", wq_s, bqs), ("k", wk_s, bks)):
                for t5 in range(NT):
                    pq = p[qgrp[0] % 2]
                    if len(qev_hist) >= 2:
                        P.wait("tensor", c_ev, qev_hist[-2])
                    for k in range(KC):
                        top(lambda t, k=k, t5=t5, wsb=wsb, pq=pq: t.matmul(
                            pq[:, :], wsb[:, k * 128:(k + 1) * 128],
                            pool1[:, k * seq + t5 * 512:k * seq + (t5 + 1) * 512],
                            start=(k == 0), stop=(k == KC - 1)),
                            c_mm if k == KC - 1 else None)
                    dstc = (B_QT[m] if which == "q" else B_KT[m]) + t5 * 512
                    P.wait("vector", c_mm, n["c_mm"])
                    if which == "q":
                        vop(lambda v, dstc=dstc, bias=bias, pr=pr, pq=pq:
                            v.tensor_scalar(
                                out=pool2[:, dstc:dstc + 512], in0=pq[:, :],
                                scalar1=bias[:, pr:pr + 1], scalar2=0.125,
                                op0=ALU.add, op1=ALU.mult), c_ev)
                    else:
                        vop(lambda v, dstc=dstc, bias=bias, pr=pr, pq=pq:
                            v.tensor_scalar(
                                out=pool2[:, dstc:dstc + 512], in0=pq[:, :],
                                scalar1=bias[:, pr:pr + 1], scalar2=None,
                                op0=ALU.add), c_ev)
                    qgrp[0] += 1
                    qev_hist.append(n["c_ev"])
            for tc in range(TC):
                pq = p[qgrp[0] % 2]
                if len(qev_hist) >= 2:
                    P.wait("tensor", c_ev, qev_hist[-2])
                for k in range(KC):
                    top(lambda t, k=k, tc=tc, wv_s=wv_s, pq=pq: t.matmul(
                        pq[:, 0:128],
                        pool1[:, k * seq + tc * 128:k * seq + (tc + 1) * 128],
                        wv_s[:, k * 128:(k + 1) * 128],
                        start=(k == 0), stop=(k == KC - 1)),
                        c_mm if k == KC - 1 else None)
                P.wait("vector", c_mm, n["c_mm"])
                vop(lambda v, tc=tc, pr=pr, pq=pq, m=m: v.tensor_tensor(
                    out=pool2[:, B_V[m] + tc * 128:B_V[m] + (tc + 1) * 128],
                    in0=pq[:, 0:128], in1=bvb[:, pr * 128:(pr + 1) * 128],
                    op=ALU.add), c_ev)
                qgrp[0] += 1
                qev_hist.append(n["c_ev"])
            qkv_ev[pr] = n["c_ev"]

        def emit_S(m, pr, qt, segs, jlist, tp_done):
            T = STM[m]
            for j in jlist:
                off, ln = segs[j]
                span = (qt + 1) * 128
                jj = segctr[0] % 2
                segctr[0] += 1
                diag = (off + ln == span)
                if len(exp_hist) >= 2:
                    P.wait("tensor", c_exp, exp_hist[-2])
                for hh in range(2):
                    inc_here = (c_mm if (hh == 1 and not diag) else None)
                    top(lambda t, qt=qt, off=off, ln=ln, jj=jj, hh=hh, m=m,
                        diag=diag: t.matmul(
                            p[jj * 2 + hh][:, 0:ln],
                            pool2[hh * 64:(hh + 1) * 64,
                                  B_QT[m] + qt * 128:B_QT[m] + (qt + 1) * 128],
                            pool2[hh * 64:(hh + 1) * 64,
                                  B_KT[m] + off:B_KT[m] + off + ln],
                            start=True, stop=not diag,
                            tile_position=(hh * 64, 0)), inc_here)
                    if diag:
                        top(lambda t, jj=jj, hh=hh, ln=ln: t.matmul(
                            p[jj * 2 + hh][:, ln - 128:ln], iden[:], maskb[:],
                            start=False, stop=True),
                            c_mm if hh == 1 else None)
                P.wait("vector", c_mm, n["c_mm"])
                for hh in range(2):
                    vop(lambda v, jj=jj, j=j, hh=hh, ln=ln, T=T: v.reduce_max(
                        out=T["nmx"][:, hh * 8 + j:hh * 8 + j + 1],
                        in_=p[jj * 2 + hh][:, 0:ln], axis=AX.X, negate=True),
                        c_nm)
                P.wait("scalar", c_nm, n["c_nm"])
                if qt >= 1:
                    P.wait("scalar", c_tp, tp_done[qt - 1])
                for hh in range(2):
                    pb, base = Pbuf(m, hh)
                    P.op("scalar", lambda s, j=j, jj=jj, hh=hh, off=off,
                         ln=ln, base=base, pb=pb, T=T: s.activation(
                             out=pb[:, base + off:base + off + ln],
                             in_=p[jj * 2 + hh][:, 0:ln], func=AF.Exp,
                             bias=T["nmx"][:, hh * 8 + j:hh * 8 + j + 1],
                             accum_out=T["ssm"][:, hh * 8 + j:hh * 8 + j + 1]),
                         inc=[(c_exp, 1)] if hh == 1 else None)
                n["c_exp"] += 1
                exp_hist.append(n["c_exp"])

        def emit_stats(m, qt, segs, nseg):
            T = STM[m]
            nmxT = T["nmx"]; ssmT = T["ssm"]; negmT = T["negm"]
            ddvT = T["ddv"]; ccvT = T["ccv"]; Z2T = T["Z2"]; rz2T = T["rz2"]
            P.wait("vector", c_exp, n["c_exp"])
            P.wait("vector", c_nm, n["c_nm"])
            if nseg == 1:
                for hh in range(2):
                    vch(lambda v, hh=hh, ccvT=ccvT, ssmT=ssmT: v.reciprocal(
                        out=ccvT[:, hh * 8:hh * 8 + 1],
                        in_=ssmT[:, hh * 8:hh * 8 + 1]))
                P.wait("vector", c_st, n["c_st"])
            else:
                for hh in range(2):
                    P.op("vector", lambda v, hh=hh, nseg=nseg, nmxT=nmxT,
                         negmT=negmT: v.tensor_reduce(
                             out=negmT[:, hh:hh + 1],
                             in_=nmxT[:, hh * 8:hh * 8 + nseg], axis=AX.X,
                             op=ALU.min), inc=[(c_st, 1)])
                    n["c_st"] += 1
                for hh in range(2):
                    sch(lambda s, hh=hh, nseg=nseg, ccvT=ccvT, nmxT=nmxT,
                        negmT=negmT: s.activation(
                            out=ccvT[:, hh * 8:hh * 8 + nseg],
                            in_=nmxT[:, hh * 8:hh * 8 + nseg], func=AF.Exp,
                            scale=-1.0, bias=negmT[:, hh:hh + 1]))
                P.wait("vector", c_st, n["c_st"])
                for hh in range(2):
                    vop(lambda v, hh=hh, nseg=nseg, ddvT=ddvT, ssmT=ssmT,
                        ccvT=ccvT: v.tensor_tensor(
                            out=ddvT[:, hh * 8:hh * 8 + nseg],
                            in0=ssmT[:, hh * 8:hh * 8 + nseg],
                            in1=ccvT[:, hh * 8:hh * 8 + nseg], op=ALU.mult), c_st)
                P.wait("vector", c_st, n["c_st"])
                for hh in range(2):
                    P.op("vector", lambda v, hh=hh, nseg=nseg, Z2T=Z2T,
                         ddvT=ddvT: v.tensor_reduce(
                             out=Z2T[:, hh:hh + 1],
                             in_=ddvT[:, hh * 8:hh * 8 + nseg], axis=AX.X,
                             op=ALU.add), inc=[(c_st, 1)])
                    n["c_st"] += 1
                P.wait("vector", c_st, n["c_st"])
                vch(lambda v, rz2T=rz2T, Z2T=Z2T: v.reciprocal(out=rz2T, in_=Z2T))
                P.wait("vector", c_st, n["c_st"])
                for hh in range(2):
                    vop(lambda v, hh=hh, nseg=nseg, ccvT=ccvT, rz2T=rz2T:
                        v.tensor_scalar(
                            out=ccvT[:, hh * 8:hh * 8 + nseg],
                            in0=ccvT[:, hh * 8:hh * 8 + nseg],
                            scalar1=rz2T[:, hh:hh + 1], scalar2=None,
                            op0=ALU.mult), c_st)
                P.wait("vector", c_st, n["c_st"])
            for j2, (off, ln) in enumerate(segs):
                for hh in range(2):
                    pb, base = Pbuf(m, hh)
                    vop(lambda v, j2=j2, hh=hh, off=off, ln=ln, base=base,
                        pb=pb, ccvT=ccvT: v.tensor_scalar(
                            out=pb[:, base + off:base + off + ln],
                            in0=pb[:, base + off:base + off + ln],
                            scalar1=ccvT[:, hh * 8 + j2:hh * 8 + j2 + 1],
                            scalar2=None, op0=ALU.mult), c_psc)

        def emit_TAV(m, pr, qt):
            span = (qt + 1) * 128
            nch = span // 128
            p4b = pp[2][:].bitcast(BF16)
            batches = []
            for hh in range(2):
                for c0 in range(0, nch, 8):
                    batches.append((hh, list(range(c0, min(c0 + 8, nch)))))

            def emit_av(hh, cb, par, pte_need):
                P.wait("tensor", c_pte, pte_need)
                P.wait("tensor", c_oev, n["c_oev"])
                for ci, c in enumerate(cb):
                    first, last = (c == 0), (c == nch - 1)
                    blast = (ci == len(cb) - 1)
                    top(lambda t, hh=hh, c=c, ci=ci, first=first, m=m,
                        last=last, par=par: t.matmul(
                            p[6][hh * 64:(hh + 1) * 64, 0:128],
                            pool2[:, B_V[m] + c * 128 + hh * 64:
                                  B_V[m] + c * 128 + hh * 64 + 64],
                            pool2[:, C_PT + par * 1024 + ci * 128:
                                  C_PT + par * 1024 + (ci + 1) * 128],
                            start=first, stop=last,
                            tile_position=(0, hh * 64)),
                        c_avb if blast else None)

            prev_b = None
            for (hh, cb) in batches:
                pb, base = Pbuf(m, hh)
                par = tb[0] % 2
                tb[0] += 1
                P.wait("tensor", c_psc, n["c_psc"])
                P.wait("tensor", c_pte, max(0, n["c_pte"] - 1))
                for ci, c in enumerate(cb):
                    top(lambda t, c=c, ci=ci, base=base, p4b=p4b, pb=pb,
                        par=par: t.matmul(
                            p4b[:, par * 1024 + ci * 128:par * 1024 + (ci + 1) * 128],
                            pb[:, base + c * 128:base + (c + 1) * 128],
                            iden[:], start=True, stop=True,
                            is_transpose=True),
                        c_tp if ci == len(cb) - 1 else None)
                P.wait("vector", c_tp, n["c_tp"])
                P.wait("vector", c_avb, max(0, n["c_avb"] - 1))
                vop(lambda v, nb=len(cb), p4b=p4b, par=par: v.tensor_copy(
                    out=pool2[:, C_PT + par * 1024:C_PT + par * 1024 + nb * 128],
                    in_=p4b[:, par * 1024:par * 1024 + nb * 128]), c_pte)
                if prev_b is not None:
                    emit_av(*prev_b)
                prev_b = (hh, cb, par, n["c_pte"])
            if prev_b is not None:
                emit_av(*prev_b)
            sl = ost[0] % 4
            ost[0] += 1
            P.wait("vector", c_avb, n["c_avb"])
            P.wait("vector", h_o[sl], 16 * ouse[sl])
            vop(lambda v, sl=sl: v.tensor_copy(
                out=ostg[:, sl * 128:(sl + 1) * 128],
                in_=p[6][:, 0:128]), c_oev)
            P.wait("sync", c_oev, n["c_oev"])
            sdma(os_d[pr][:, qt * 128:(qt + 1) * 128],
                 ostg[:, sl * 128:(sl + 1) * 128], h_o[sl])
            ouse[sl] += 1

        for duo in range(NPAIR // 2):
            prA, prB = 2 * duo, 2 * duo + 1
            emit_qkv(0, prA)
            if prA + 2 < NPAIR:
                load_pair_w(prA + 2, qkv_ev[prA])
            emit_qkv(1, prB)
            if prB + 2 < NPAIR:
                load_pair_w(prB + 2, qkv_ev[prB])

            tpd = [[0] * QT, [0] * QT]
            acc_tp = n["c_tp"]
            for qt in range(1, QT + 1):
                for m in range(2):
                    q = qt - 1
                    if qt <= QT - 1 or True:
                        pass
            order = []
            for qt in range(1, QT):
                order.append((0, qt - 1))
                order.append((1, qt - 1))
            order.append((0, QT - 1))
            order.append((1, QT - 1))
            for (m, q) in order:
                acc_tp += 2 * ((q // 8) + 1)
                tpd[m][q] = acc_tp

            P.wait("tensor", c_ev, qkv_ev[prB])
            for qt in range(QT):
                span = (qt + 1) * 128
                nseg = (span + SEG - 1) // SEG
                segs = [(j * SEG, min(SEG, span - j * SEG)) for j in range(nseg)]
                emit_S(0, prA, qt, segs, list(range(min(2, nseg))), tpd[0])
                if qt >= 1:
                    emit_TAV(0, prA, qt - 1)
                emit_S(0, prA, qt, segs, list(range(2, nseg)), tpd[0])
                emit_S(1, prB, qt, segs, list(range(min(2, nseg))), tpd[1])
                if qt >= 1:
                    emit_TAV(1, prB, qt - 1)
                emit_S(1, prB, qt, segs, list(range(2, nseg)), tpd[1])
                emit_stats(0, qt, segs, nseg)
                emit_stats(1, qt, segs, nseg)
            emit_TAV(0, prA, QT - 1)
            emit_TAV(1, prB, QT - 1)

        for s_o in (h_o0, h_o1, h_o2, h_o3):
            P.wait("gpsimd", s_o, n[s_o.name])
        P.wait("gpsimd", c_avb, n["c_avb"])
        for pr in range(NPAIR):
            gdma(pool2[:, pr * 1024:(pr + 1) * 1024], wo_d[pr], g_wo)
        ATT_W = n["g_wo"]
        yb = pool3[:, 4096:8192]
        ysq = pool3[:, 8192:12288]
        xt2 = ft[:, FT_X:FT_X + 4096]
        P.wait("tensor", c_exp, n["c_exp"])
        P.wait("tensor", c_oev, n["c_oev"])
        P.wait("tensor", c_pte, n["c_pte"])
        P.wait("tensor", c_ev, n["c_ev"])
        for ts in range(NT):
            P.wait("gpsimd", c_ev, n["c_ev"])
            P.wait("gpsimd", c_st, n["c_st"])
            P.wait("gpsimd", h_xw, n["h_xw"])
            for pr in range(NPAIR):
                gdma(pool3[:, pr * 512:(pr + 1) * 512],
                     os_d[pr][:, ts * 512:(ts + 1) * 512], g_or)
            gdma(xt2.rearrange("p (k t) -> p k t", k=KC),
                 out_v[:, :, ts * 512:(ts + 1) * 512], g_xb)
            P.wait("tensor", g_wo, ATT_W)
            P.wait("tensor", g_or, n["g_or"])
            P.wait("tensor", c_st, n["c_st"])
            P.wait("vector", c_st, n["c_st"])
            for d in range(KC):
                P.wait("tensor", c_ev, max(0, n["c_ev"] - 1))
                for pr in range(NPAIR):
                    top(lambda t, d=d, pr=pr: t.matmul(
                        p[d % 2][:, :],
                        pool2[:, pr * 1024 + d * 128:pr * 1024 + (d + 1) * 128],
                        pool3[:, pr * 512:(pr + 1) * 512],
                        start=(pr == 0), stop=(pr == NPAIR - 1)),
                        c_mm if pr == NPAIR - 1 else None)
                P.wait("vector", c_mm, n["c_mm"])
                vop(lambda v, d=d: v.tensor_scalar(
                    out=yb[:, d * 512:(d + 1) * 512], in0=p[d % 2][:, :],
                    scalar1=bos[:, d:d + 1], scalar2=None, op0=ALU.add), c_ev)
            P.wait("scalar", c_ev, n["c_ev"])
            P.wait("scalar", c_mm, n["c_mm"])
            P.op("scalar", lambda s: s.activation(out=ysq, in_=yb, func=AF.Square),
                 inc=[(c_ln, 1)]); n["c_ln"] += 1
            P.wait("tensor", c_ln, n["c_ln"])
            P.wait("tensor", c_st, n["c_st"])
            stats_mms(yb, ysq)
            P.wait("vector", c_mm, n["c_mm"])
            r_mu, r_rs = ln_stats_and_rstd()
            P.wait("tensor", c_st, n["c_st"])
            bcast_mms(r_mu, r_rs)
            P.wait("vector", c_mm, n["c_mm"])
            P.wait("vector", g_xb, n["g_xb"])
            for k in range(KC):
                stg = ft[:, 3584:3584 + 512]
                vch(lambda v, k=k, stg=stg: v.tensor_tensor(
                    out=stg, in0=yb[:, k * 512:(k + 1) * 512], in1=p[5][:, :],
                    op=ALU.subtract))
                vch(lambda v, stg=stg: v.tensor_tensor(
                    out=stg, in0=stg, in1=p[6][:, :], op=ALU.mult))
                sch(lambda s, k=k, stg=stg: s.activation(
                    out=stg, in_=stg, func=AF.Identity,
                    scale=gams[:, k:k + 1], bias=bets[:, k:k + 1]))
                vch(lambda v, k=k, stg=stg: v.tensor_tensor(
                    out=xt2[:, k * 512:(k + 1) * 512],
                    in0=xt2[:, k * 512:(k + 1) * 512], in1=stg, op=ALU.add))
            P.wait("sync", c_st, n["c_st"])
            sdma(out_v[:, :, ts * 512:(ts + 1) * 512],
                 xt2.rearrange("p (k t) -> p k t", k=KC), h_xw)

    ffn_call(0, h_pre, 16, 0)
    XW0 = n["h_xw"]
    attention()
    ffn_call(1, h_xw, XW0, 16)
    P.wait("gpsimd", g_acca, n["g_acca"])
    P.wait("gpsimd", g_accb, n["g_accb"])
    P.wait("sync", h_xw, n["h_xw"])
    P.wait("sync", h_pre, PRE_H)
    P.wait("gpsimd", g_pre, PRE_G)

    P.check()

    blk = st.enter_context(nc.Block())

    @blk.gpsimd
    def _(g):
        P.emit("gpsimd", g)

    @blk.tensor
    def _(t):
        P.emit("tensor", t)

    @blk.scalar
    def _(s):
        P.emit("scalar", s)

    @blk.vector
    def _(v):
        P.emit("vector", v)

    @blk.sync
    def _(sy):
        P.emit("sync", sy)

    st.close()
    return nc


def _layouts(inputs):
    bf = ml_dtypes.bfloat16
    w1 = inputs["L0_ffn_w1"]
    w2 = inputs["L0_ffn_w2"]
    w1r = np.ascontiguousarray(
        w1.reshape(E, KC, 128, MC, 128).transpose(0, 2, 3, 1, 4))
    w2r = np.ascontiguousarray(w2.reshape(E, MC, 128, D).transpose(0, 2, 1, 3))
    b1r = np.ascontiguousarray(
        inputs["L0_ffn_b1"].reshape(E, MC, 128).transpose(2, 0, 1).reshape(128, E * MC))
    b2r = np.ascontiguousarray(inputs["L0_ffn_b2"])
    gwT = np.ascontiguousarray(
        inputs["L0_ffn_gate_w"].T.reshape(KC, 128, 8))
    gbb = np.ascontiguousarray(np.broadcast_to(inputs["L0_ffn_gate_b"], (128, 8)))

    def qk_r(w):
        return np.ascontiguousarray(
            w.reshape(KC, 128, NPAIR, 128).transpose(2, 1, 0, 3))

    common = dict(
        w1r=w1r, w2r=w2r, b1r=b1r, b2r=b2r, gwT=gwT, gbb=gbb,
        wqr=qk_r(inputs["L0_attn_Wq"]),
        wkr=qk_r(inputs["L0_attn_Wk"]),
        wvr=qk_r(inputs["L0_attn_Wv"]),
        wor=np.ascontiguousarray(inputs["L0_attn_Wo"].reshape(NPAIR, 128, D)),
        bqr=np.ascontiguousarray(inputs["L0_attn_bq"].reshape(NPAIR, 128).T),
        bkr=np.ascontiguousarray(inputs["L0_attn_bk"].reshape(NPAIR, 128).T),
        bvbr=np.ascontiguousarray(np.broadcast_to(
            inputs["L0_attn_bv"].reshape(1, NPAIR * 128),
            (128, NPAIR * 128))).astype(bf),
        bor=np.ascontiguousarray(inputs["L0_attn_bo"].reshape(KC, 128).T),
        gamr=np.ascontiguousarray(inputs["L0_attn_gamma"].reshape(KC, 128).T),
        betr=np.ascontiguousarray(inputs["L0_attn_beta"].reshape(KC, 128).T),
        idenb=np.eye(128, dtype=np.float32).astype(bf),
        maskb=np.where(np.tril(np.ones((128, 128), dtype=bool)), 0.0,
                       -1e30).astype(np.float32).astype(bf),
        onesb=np.ones((128, 1), dtype=np.float32).astype(bf),
        onesf=np.ones((1, 128), dtype=np.float32),
    )
    eselb = np.zeros((8, E * 128), dtype=np.float32)
    for e in range(E):
        eselb[e, e * 128:(e + 1) * 128] = 1.0
    common["eselb"] = eselb.astype(bf)
    return common


def kernel(**inputs):
    x = inputs["L0_attn_x"]
    seq = x.shape[1]
    common = _layouts(inputs)
    nc = build_program(seq)
    in_maps = []
    for c in range(x.shape[0]):
        m = dict(common)
        m["x"] = np.ascontiguousarray(x[c].T)
        in_maps.append(m)
    res = run_bass_kernel_spmd(nc, in_maps, core_ids=list(range(x.shape[0])))
    outs = [np.asarray(r["out"]).T for r in res.results]
    return np.stack(outs, axis=0).astype(np.float32)


# version 16 (final)
# speedup vs baseline: 1.3043x; 1.1446x over previous
import contextlib
import numpy as np
import ml_dtypes
import concourse.bass as bass
import concourse.mybir as mybir
from concourse.bass_utils import run_bass_kernel_spmd

F32 = mybir.dt.float32
BF16 = mybir.dt.bfloat16
AF = mybir.ActivationFunctionType
ALU = mybir.AluOpType
AX = mybir.AxisListType

D = 1024
HID = 4096
E = 8
KC = D // 128
MC = HID // 128
NCORES = 8
NPAIR = 8
EPS = 1e-5
SEG = 512


class Prog:
    ENGS = ("tensor", "scalar", "vector", "gpsimd", "sync")

    def __init__(self):
        self.streams = {e: [] for e in self.ENGS}

    def wait(self, eng, sem, count):
        if count > 0:
            self.streams[eng].append(("w", sem, count))

    def op(self, eng, fn, inc=None):
        self.streams[eng].append(("o", fn, inc or []))

    def check(self):
        sems, pos = {}, {e: 0 for e in self.ENGS}
        progress = True
        while progress:
            progress = False
            for e in self.ENGS:
                while pos[e] < len(self.streams[e]):
                    it = self.streams[e][pos[e]]
                    if it[0] == "w":
                        if sems.get(it[1].name, 0) < it[2]:
                            break
                    else:
                        for sem, k in it[2]:
                            sems[sem.name] = sems.get(sem.name, 0) + k
                    pos[e] += 1
                    progress = True
        stuck = [(e, pos[e], len(self.streams[e])) for e in self.ENGS
                 if pos[e] < len(self.streams[e])]
        if stuck:
            parts = []
            for e, pp, nn_ in stuck:
                it = self.streams[e][pp]
                parts.append(f"{e}@{pp}/{nn_} waits {it[1].name}>={it[2]} "
                             f"(have {sems.get(it[1].name, 0)})")
            raise RuntimeError("Prog deadlock: " + "; ".join(parts))

    def emit(self, eng_name, eng):
        for it in self.streams[eng_name]:
            if it[0] == "w":
                eng.wait_ge(it[1], it[2])
            else:
                ins = it[1](eng)
                for sem, k in it[2]:
                    ins.then_inc(sem, k)


def build_program(seq):
    NT = seq // 512
    QT = seq // 128
    TC = seq // 128

    nc = bass.Bass("TRN2", target_bir_lowering=False)
    dr = lambda name, s, d, k: nc.dram_tensor(name, s, d, kind=k).ap()

    x_d = dr("x", [D, seq], F32, "ExternalInput")
    out_d = dr("out", [D, seq], F32, "ExternalOutput")
    w1_d = dr("w1r", [E, 128, MC, KC, 128], BF16, "ExternalInput")
    w2_d = dr("w2r", [E, 128, MC, D], BF16, "ExternalInput")
    b1_d = dr("b1r", [128, E * MC], F32, "ExternalInput")
    b2_d = dr("b2r", [8, D], BF16, "ExternalInput")
    gwT_d = dr("gwT", [KC, 128, 8], BF16, "ExternalInput")
    gbb_d = dr("gbb", [128, 8], F32, "ExternalInput")
    wq_d = dr("wqr", [NPAIR, 128, KC, 128], BF16, "ExternalInput")
    wk_d = dr("wkr", [NPAIR, 128, KC, 128], BF16, "ExternalInput")
    wv_d = dr("wvr", [NPAIR, 128, KC, 128], BF16, "ExternalInput")
    wo_d = dr("wor", [NPAIR, 128, D], BF16, "ExternalInput")
    bq_d = dr("bqr", [128, NPAIR], F32, "ExternalInput")
    bk_d = dr("bkr", [128, NPAIR], F32, "ExternalInput")
    bvb_d = dr("bvbr", [128, NPAIR * 128], BF16, "ExternalInput")
    bo_d = dr("bor", [128, KC], F32, "ExternalInput")
    gam_d = dr("gamr", [128, KC], F32, "ExternalInput")
    bet_d = dr("betr", [128, KC], F32, "ExternalInput")
    iden_d = dr("idenb", [128, 128], BF16, "ExternalInput")
    maskd_d = dr("maskdb", [128, 2048], BF16, "ExternalInput")
    vtmpl_d = dr("vtmplb", [128, 4160], BF16, "ExternalInput")
    ones_d = dr("onesb", [128, 1], BF16, "ExternalInput")
    onesf_d = dr("onesf", [1, 128], F32, "ExternalInput")
    esel_d = dr("eselb", [8, E * 128], BF16, "ExternalInput")
    xlns_d = dr("xlns", [D, seq], BF16, "Internal")
    os_d = dr("osp", [NPAIR, 128, seq], BF16, "Internal")

    st = contextlib.ExitStack()
    sb = lambda name, s, d: st.enter_context(nc.sbuf_tensor(name, s, d))

    pool1 = sb("pool1", [128, 32768], BF16)
    pool2 = sb("pool2", [128, 32768], BF16)
    pool3 = sb("pool3", [128, 16384], BF16)
    ft = sb("ft", [128, 8192], F32)
    w8 = sb("w8", [8, seq], BF16)
    b1t = sb("b1t", [128, E * MC], F32)
    b2t = sb("b2t", [8, D], BF16)
    gwT = sb("gwTs", [128, KC * 8], BF16)
    gbb = sb("gbbs", [128, 8], F32)
    bqs = sb("bqs", [128, NPAIR], F32)
    bks = sb("bks", [128, NPAIR], F32)
    bvb = sb("bvbs", [128, NPAIR * 128], BF16)
    bos = sb("boss", [128, KC], F32)
    gams = sb("gams", [128, KC], F32)
    bets = sb("betss", [128, KC], F32)
    iden = sb("idens", [128, 128], BF16)
    onesb = sb("onesbs", [128, 1], BF16)
    onesf = sb("onesfs", [1, 128], F32)
    stat = sb("stat", [128, 32], F32)
    esel = ft[:].bitcast(BF16)[0:8, 6144:6144 + E * 128]

    FT_STG = (0, 512)
    FT_WB = (1024, 1536)
    FT_ROW = 2048
    FT_STX = 3584 + 512
    FT_STX = 3840
    FT_X = 4096
    FT_Y = 0
    rowsB = ft
    stx = ft

    pp = [st.enter_context(nc.psum_tensor(f"pp{i}", [128, 1024], F32))
          for i in range(4)]
    p = [pp[i // 2][:, (i % 2) * 512:(i % 2 + 1) * 512] for i in range(8)]

    sem = lambda name: st.enter_context(nc.semaphore(name))
    g_w1 = sem("g_w1"); g_w2 = sem("g_w2"); g_xb = sem("g_xb")
    g_acca = sem("g_acca"); g_accb = sem("g_accb")
    g_att0 = sem("g_att0"); g_att1 = sem("g_att1"); g_wo = sem("g_wo"); g_or = sem("g_or")
    g_pre = sem("g_pre")
    h_xlw = sem("h_xlw"); h_xlr0 = sem("h_xlr0"); h_xlr1 = sem("h_xlr1"); h_ow = sem("h_ow")
    h_o0 = sem("h_o0"); h_o1 = sem("h_o1"); h_o2 = sem("h_o2"); h_o3 = sem("h_o3")
    h_xw = sem("h_xw"); h_pre = sem("h_pre")
    c_h = sem("c_h"); c_gel = sem("c_gel"); c_o = sem("c_o"); c_ev = sem("c_ev")
    c_wb = sem("c_wb"); c_wbe = sem("c_wbe"); c_ln = sem("c_ln")
    c_st = sem("c_st"); c_g = sem("c_g"); c_mm = sem("c_mm")
    c_nm = sem("c_nm"); c_exp = sem("c_exp"); c_psc = sem("c_psc")
    c_tp = sem("c_tp"); c_pte = sem("c_pte"); c_avb = sem("c_avb")
    c_oev = sem("c_oev")

    P = Prog()
    nstage = [0]
    n = {}
    for s_ in (g_w1, g_w2, g_xb, g_acca, g_accb, g_att0, g_att1, g_wo, g_or,
               g_pre, h_xlw, h_xlr0, h_xlr1, h_ow, h_o0, h_o1, h_o2, h_o3,
               h_xw, h_pre, c_h, c_gel,
               c_o, c_ev, c_wb, c_wbe, c_ln, c_st, c_g, c_mm, c_nm, c_exp,
               c_psc, c_tp, c_pte, c_avb, c_oev):
        n[s_.name] = 0

    def gdma(dst, src, s, accum=False):
        if accum:
            P.op("gpsimd", lambda g, dst=dst, src=src: g.dma_start(
                out=dst, in_=src, accum_op=ALU.add), inc=[(s, 16)])
        else:
            P.op("gpsimd", lambda g, dst=dst, src=src: g.dma_start(
                out=dst, in_=src), inc=[(s, 16)])
        n[s.name] += 16

    def sdma(dst, src, s):
        P.op("sync", lambda sy, dst=dst, src=src: sy.dma_start(out=dst, in_=src),
             inc=[(s, 16)])
        n[s.name] += 16

    def top(fn, s=None):
        inc = [(s, 1)] if s is not None else []
        P.op("tensor", fn, inc=inc)
        if s is not None:
            n[s.name] += 1

    def vop(fn, s=None):
        inc = [(s, 1)] if s is not None else []
        P.op("vector", fn, inc=inc)
        if s is not None:
            n[s.name] += 1

    def vch(fn):
        P.wait("vector", c_st, n["c_st"])
        P.op("vector", fn, inc=[(c_st, 1)])
        n["c_st"] += 1

    def sch(fn):
        P.wait("scalar", c_st, n["c_st"])
        P.op("scalar", fn, inc=[(c_st, 1)])
        n["c_st"] += 1

    out_v = out_d.rearrange("(k p) t -> p k t", p=128)
    xlns_v = xlns_d.rearrange("(k p) t -> p k t", p=128)

    for dst, src in ((b1t, b1_d), (gbb, gbb_d), (bqs, bq_d), (bks, bk_d),
                     (bvb, bvb_d), (bos, bo_d), (gams, gam_d), (bets, bet_d),
                     (onesf, onesf_d), (b2t, b2_d), (iden, iden_d),
                     (onesb, ones_d), (esel, esel_d)):
        gdma(dst[:], src[:], g_pre)
    gdma(gwT[:].rearrange("p (k j) -> p k j", k=KC),
         gwT_d.rearrange("k i j -> i k j"), g_pre)
    sdma(out_d[:], x_d[:], h_pre)
    PRE_G = n["g_pre"]
    PRE_H = n["h_pre"]
    P.wait("tensor", g_pre, PRE_G)
    P.wait("scalar", g_pre, PRE_G)
    P.wait("vector", g_pre, PRE_G)

    def ln_stats_and_rstd():
        r_mu = rowsB[0:1, 1536:2048]
        r_rs = rowsB[0:1, 2048:2560]
        r_t1 = rowsB[0:1, 3584:4096]
        vch(lambda v: v.tensor_scalar(out=r_mu, in0=p[5][0:1, :],
                                      scalar1=1.0 / D, scalar2=None, op0=ALU.mult))
        vch(lambda v: v.tensor_scalar(out=r_rs, in0=p[6][0:1, :],
                                      scalar1=1.0 / D, scalar2=None, op0=ALU.mult))
        vch(lambda v: v.tensor_tensor(out=r_t1, in0=r_mu, in1=r_mu, op=ALU.mult))
        vch(lambda v: v.tensor_tensor(out=r_rs, in0=r_rs, in1=r_t1, op=ALU.subtract))
        vch(lambda v: v.tensor_scalar(out=r_rs, in0=r_rs, scalar1=EPS,
                                      scalar2=None, op0=ALU.add))
        sch(lambda s: s.activation(out=r_rs, in_=r_rs, func=AF.Sqrt))
        vch(lambda v: v.reciprocal(out=r_rs, in_=r_rs))
        return r_mu, r_rs

    def stats_mms(src_a, src_b):
        for k in range(KC):
            top(lambda t, k=k: t.matmul(
                p[5][0:1, :], onesb[:], src_a[:, k * 512:(k + 1) * 512],
                start=(k == 0), stop=(k == KC - 1)))
            top(lambda t, k=k: t.matmul(
                p[6][0:1, :], onesb[:], src_b[:, k * 512:(k + 1) * 512],
                start=(k == 0), stop=(k == KC - 1)),
                c_mm if k == KC - 1 else None)

    def bcast_mms(r_mu, r_rs):
        top(lambda t: t.matmul(p[5][:, :], onesf[:], r_mu, start=True, stop=True))
        top(lambda t: t.matmul(p[6][:, :], onesf[:], r_rs, start=True, stop=True),
            c_mm)

    def ffn_call(call, xdep_sem, xdep_base, xdep_step):
        xb = pool3[:, 0:4096]
        xsq = pool3[:, 4096:8192]
        xlt = pool3[:, 8192:12288]
        gtile = pool3[:, 12288:12296]
        ev_hist = []

        for ts in range(NT):
            P.wait("gpsimd", c_st, n["c_st"])
            P.wait("gpsimd", xdep_sem, xdep_base + xdep_step * (ts + 1))
            gdma(xb.rearrange("p (k t) -> p k t", k=KC),
                 out_v[:, :, ts * 512:(ts + 1) * 512], g_xb)
            P.wait("scalar", g_xb, n["g_xb"])
            P.wait("scalar", c_mm, n["c_mm"])
            P.op("scalar", lambda s: s.activation(out=xsq, in_=xb, func=AF.Square),
                 inc=[(c_ln, 1)]); n["c_ln"] += 1
            P.wait("tensor", c_ln, n["c_ln"])
            P.wait("tensor", c_st, n["c_st"])
            stats_mms(xb, xsq)
            P.wait("vector", c_mm, n["c_mm"])
            r_mu, r_rs = ln_stats_and_rstd()
            P.wait("tensor", c_st, n["c_st"])
            bcast_mms(r_mu, r_rs)
            P.wait("vector", c_mm, n["c_mm"])
            P.wait("vector", c_g, n["c_g"])
            P.wait("vector", h_xlw, n["h_xlw"])
            for k in range(KC):
                stg = ft[:, 3584:3584 + 512]
                vch(lambda v, k=k, stg=stg: v.tensor_tensor(
                    out=stg, in0=xb[:, k * 512:(k + 1) * 512], in1=p[5][:, :],
                    op=ALU.subtract))
                vch(lambda v, k=k, stg=stg: v.tensor_tensor(
                    out=xlt[:, k * 512:(k + 1) * 512], in0=stg, in1=p[6][:, :],
                    op=ALU.mult))
            P.wait("sync", c_st, n["c_st"])
            P.wait("sync", h_xlr0, n["h_xlr0"])
            P.wait("sync", h_xlr1, n["h_xlr1"])
            sdma(xlns_v[:, :, ts * 512:(ts + 1) * 512],
                 xlt.rearrange("p (k t) -> p k t", k=KC), h_xlw)
            for g4 in range(4):
                tok0 = g4 * 128
                P.wait("tensor", c_g, n["c_g"])
                P.wait("tensor", c_pte, n["c_pte"])
                P.wait("tensor", c_wbe, n["c_wbe"])
                P.wait("tensor", c_st, n["c_st"])
                for k in range(KC):
                    top(lambda t, k=k, tok0=tok0: t.matmul(
                        p[4][:, 0:8],
                        xlt[:, k * 512 + tok0:k * 512 + tok0 + 128],
                        gwT[:, k * 8:(k + 1) * 8],
                        start=(k == 0), stop=(k == KC - 1)),
                        c_mm if k == KC - 1 else None)
                gg = stat[:, 0:8]; m1 = stat[:, 8:9]; m2 = stat[:, 9:10]
                msk1 = stat[:, 10:18]; tmp8 = stat[:, 18:26]
                e21 = stat[:, 26:27]; wA = stat[:, 27:28]; wB = stat[:, 28:29]
                P.wait("vector", c_mm, n["c_mm"])
                vch(lambda v: v.tensor_tensor(out=gg, in0=p[4][:, 0:8],
                                              in1=gbb[:], op=ALU.add))
                vch(lambda v: v.reduce_max(out=m1, in_=gg, axis=AX.X))
                vch(lambda v: v.tensor_scalar(out=msk1, in0=gg, scalar1=m1,
                                              scalar2=None, op0=ALU.is_ge))
                vch(lambda v: v.tensor_scalar(out=tmp8, in0=msk1, scalar1=1e30,
                                              scalar2=None, op0=ALU.mult))
                vch(lambda v: v.tensor_tensor(out=tmp8, in0=gg, in1=tmp8,
                                              op=ALU.subtract))
                vch(lambda v: v.reduce_max(out=m2, in_=tmp8, axis=AX.X))
                vch(lambda v: v.tensor_tensor(out=e21, in0=m2, in1=m1,
                                              op=ALU.subtract))
                sch(lambda s: s.activation(out=e21, in_=e21, func=AF.Exp))
                vch(lambda v: v.tensor_scalar(out=wA, in0=e21, scalar1=1.0,
                                              scalar2=None, op0=ALU.add))
                vch(lambda v: v.reciprocal(out=wA, in_=wA))
                vch(lambda v: v.tensor_scalar(out=wB, in0=wA, scalar1=-1.0,
                                              scalar2=1.0, op0=ALU.mult,
                                              op1=ALU.add))
                vch(lambda v: v.tensor_scalar(out=tmp8, in0=tmp8, scalar1=m2,
                                              scalar2=None, op0=ALU.is_ge))
                vch(lambda v: v.tensor_scalar(out=tmp8, in0=tmp8, scalar1=wB,
                                              scalar2=None, op0=ALU.mult))
                vch(lambda v: v.tensor_scalar(out=msk1, in0=msk1, scalar1=wA,
                                              scalar2=None, op0=ALU.mult))
                vch(lambda v: v.tensor_tensor(out=msk1, in0=msk1, in1=tmp8,
                                              op=ALU.add))
                vch(lambda v: v.tensor_scalar(out=gtile, in0=msk1, scalar1=0.5,
                                              scalar2=None, op0=ALU.mult))
                p4b = p[4][:].bitcast(BF16)
                P.wait("tensor", c_st, n["c_st"])
                top(lambda t, p4b=p4b: t.matmul(
                    p4b[0:8, 128:256], gtile, iden[:], start=True, stop=True,
                    is_transpose=True), c_mm)
                P.wait("vector", c_mm, n["c_mm"])
                vop(lambda v, ts=ts, g4=g4, p4b=p4b: v.tensor_copy(
                    out=w8[0:8, ts * 512 + g4 * 128:ts * 512 + (g4 + 1) * 128],
                    in_=p4b[0:8, 128:256]), c_g)
            for d in range(KC):
                pb = p[d % 2]
                P.wait("tensor", c_ev, max(0, n["c_ev"] - 1))
                P.wait("tensor", c_g, n["c_g"])
                top(lambda t, d=d, ts=ts, pb=pb: t.matmul(
                    pb[:, :], b2t[0:8, d * 128:(d + 1) * 128],
                    w8[0:8, ts * 512:(ts + 1) * 512], start=True, stop=True),
                    c_mm)
                sg = nstage[0] % 2
                nstage[0] += 1
                sacc = g_acca if sg == 0 else g_accb
                stg = ft[:, FT_STG[sg]:FT_STG[sg] + 512]
                P.wait("vector", c_mm, n["c_mm"])
                P.wait("vector", sacc, n[sacc.name])
                vop(lambda v, pb=pb, stg=stg: v.tensor_copy(out=stg, in_=pb[:, :]),
                    c_ev)
                P.wait("gpsimd", c_ev, n["c_ev"])
                gdma(out_d.rearrange("(c p) t -> c p t", p=128)[d, :, ts * 512:(ts + 1) * 512],
                     stg, sacc, accum=True)

        LN_ACCA = n["g_acca"]
        LN_ACCB = n["g_accb"]
        XLW_ALL = n["h_xlw"]
        W8_DONE = n["c_g"]

        xl2 = ft[:, FT_X:FT_X + 4096].bitcast(BF16)
        h_base = n["c_h"]; o_base = n["c_o"]; gel_base = n["c_gel"]
        gw1_base = n["g_w1"]; gw2_base = n["g_w2"]
        xlr_bases = (n["h_xlr0"], n["h_xlr1"])

        def load_w1(e):
            for pc in range(4):
                if e > 0:
                    P.wait("gpsimd", c_h,
                           h_base + ((e - 1) * NT + NT - 1) * MC + 8 * (pc + 1))
                P.wait("gpsimd", g_w1, gw1_base + 16 * (4 * e + pc))
                gdma(pool1[:, pc * 8192:(pc + 1) * 8192].rearrange(
                        "p (m k j) -> p m k j", m=8, k=KC),
                     w1_d[e, :, pc * 8:(pc + 1) * 8], g_w1)

        def load_w2(e):
            if e > 0:
                P.wait("gpsimd", c_o, o_base + e * NT * KC)
            gdma(pool2[:, :].rearrange("p (m dd) -> p m dd", m=MC),
                 w2_d[e], g_w2)

        P.wait("gpsimd", c_ev, n["c_ev"])
        load_w1(0)
        load_w2(0)

        P.wait("tensor", c_ev, n["c_ev"])
        for e in range(E):
            for ts in range(NT):
                idx = e * NT + ts
                sxlr = h_xlr0 if idx % 2 == 0 else h_xlr1
                if idx >= 2:
                    P.wait("sync", c_h, h_base + (idx - 1) * MC)
                P.wait("sync", h_xlw, XLW_ALL)
                sdma(xl2[:, (idx % 2) * 4096:(idx % 2 + 1) * 4096].rearrange(
                        "p (k t) -> p k t", k=KC),
                     xlns_v[:, :, ts * 512:(ts + 1) * 512], sxlr)
                xt = xl2[:, (idx % 2) * 4096:(idx % 2 + 1) * 4096]

                wbslot = idx % 2
                wbv = ft[:, FT_WB[wbslot]:FT_WB[wbslot] + 512]
                P.wait("tensor", c_wbe, max(0, n["c_wbe"] - 1))
                P.wait("tensor", c_g, W8_DONE)
                P.wait("tensor", c_pte, n["c_pte"])
                top(lambda t, e=e, ts=ts: t.matmul(
                    p[4][:, :], esel[0:8, e * 128:(e + 1) * 128],
                    w8[0:8, ts * 512:(ts + 1) * 512], start=True, stop=True),
                    c_wb)
                P.wait("vector", c_wb, n["c_wb"])
                if idx >= 2:
                    P.wait("vector", c_ev, ev_hist[idx - 2])
                vop(lambda v, wbv=wbv: v.tensor_copy(out=wbv, in_=p[4][:, :]),
                    c_wbe)

                sxlr = h_xlr0 if idx % 2 == 0 else h_xlr1
                P.wait("tensor", sxlr, xlr_bases[idx % 2] + 16 * (idx // 2 + 1))
                for m in range(MC):
                    gi = idx * MC + m
                    P.wait("tensor", g_w1, gw1_base + 16 * (4 * e + m // 8 + 1))
                    if gi >= 2:
                        P.wait("tensor", c_gel, gel_base + gi - 1)
                    for k in range(KC):
                        top(lambda t, m=m, k=k, xt=xt: t.matmul(
                            p[m % 2][:, :],
                            pool1[:, m * 1024 + k * 128:m * 1024 + (k + 1) * 128],
                            xt[:, k * 512:(k + 1) * 512],
                            start=(k == 0), stop=(k == KC - 1)),
                            c_h if k == KC - 1 else None)
                    P.wait("scalar", c_h, n["c_h"])
                    if idx > 0:
                        P.wait("scalar", c_o, o_base + idx * KC)
                    P.op("scalar", lambda s, m=m, e=e: s.activation(
                        out=pool3[:, m * 512:(m + 1) * 512], in_=p[m % 2][:, :],
                        func=AF.Gelu, bias=b1t[:, e * MC + m:e * MC + m + 1]),
                        inc=[(c_gel, 1)]); n["c_gel"] += 1
                P.wait("tensor", c_gel, gel_base + (idx + 1) * MC)
                P.wait("tensor", g_w2, gw2_base + 16 * (e + 1))
                P.wait("tensor", c_wbe, n["c_wbe"])
                for d in range(KC):
                    P.wait("tensor", c_ev, max(0, n["c_ev"] - 1))
                    for m in range(MC):
                        top(lambda t, d=d, m=m: t.matmul(
                            p[2 + d % 2][:, :],
                            pool2[:, m * 1024 + d * 128:m * 1024 + (d + 1) * 128],
                            pool3[:, m * 512:(m + 1) * 512],
                            start=(m == 0), stop=(m == MC - 1)),
                            c_o if m == MC - 1 else None)
                    sg = nstage[0] % 2
                    nstage[0] += 1
                    sacc = g_acca if sg == 0 else g_accb
                    stg = ft[:, FT_STG[sg]:FT_STG[sg] + 512]
                    P.wait("vector", c_o, n["c_o"])
                    P.wait("vector", sacc, n[sacc.name])
                    vop(lambda v, d=d, stg=stg, wbv=wbv: v.tensor_tensor(
                        out=stg, in0=p[2 + d % 2][:, :], in1=wbv, op=ALU.mult),
                        c_ev)
                    P.wait("gpsimd", c_ev, n["c_ev"])
                    if e == 0 and ts == 0 and d == 0:
                        P.wait("gpsimd", g_acca, LN_ACCA)
                        P.wait("gpsimd", g_accb, LN_ACCB)
                    gdma(out_d.rearrange("(c p) t -> c p t", p=128)[d, :, ts * 512:(ts + 1) * 512],
                         stg, sacc, accum=True)
                ev_hist.append(n["c_ev"])
            if e + 1 < E:
                load_w1(e + 1)
                load_w2(e + 1)

    def attention():
        ACC1A = n["g_acca"]
        ACC1B = n["g_accb"]
        xb = pool3[:, 0:4096]
        xsq = pool3[:, 4096:8192]
        for ts in range(NT):
            P.wait("gpsimd", c_st, n["c_st"])
            P.wait("gpsimd", g_acca, ACC1A)
            P.wait("gpsimd", g_accb, ACC1B)
            gdma(xb.rearrange("p (k t) -> p k t", k=KC),
                 out_v[:, :, ts * 512:(ts + 1) * 512], g_xb)
            P.wait("scalar", g_xb, n["g_xb"])
            P.wait("scalar", c_mm, n["c_mm"])
            P.op("scalar", lambda s: s.activation(out=xsq, in_=xb, func=AF.Square),
                 inc=[(c_ln, 1)]); n["c_ln"] += 1
            P.wait("tensor", c_ln, n["c_ln"])
            P.wait("tensor", c_st, n["c_st"])
            stats_mms(xb, xsq)
            P.wait("vector", c_mm, n["c_mm"])
            r_mu, r_rs = ln_stats_and_rstd()
            P.wait("tensor", c_st, n["c_st"])
            bcast_mms(r_mu, r_rs)
            P.wait("vector", c_mm, n["c_mm"])
            for k in range(KC):
                stg = ft[:, 3584:3584 + 512]
                vch(lambda v, k=k, stg=stg: v.tensor_tensor(
                    out=stg, in0=xb[:, k * 512:(k + 1) * 512], in1=p[5][:, :],
                    op=ALU.subtract))
                vch(lambda v, k=k, ts=ts, stg=stg: v.tensor_tensor(
                    out=pool1[:, k * seq + ts * 512:k * seq + (ts + 1) * 512],
                    in0=stg, in1=p[6][:, :], op=ALU.mult))
        XLN_DONE = n["c_st"]

        W_QKV, W_Q, W_K, W_V = 0, 6144, 14336, 22528
        maskd = pool3[:, 10240:12288]
        scnt = [0]
        gcnt = [0]
        sexp_hist = []
        sav_hist = []
        qkv_done = {}
        attw_done = {}
        tt_done = [0]
        psc_req = {}
        qg = [0]
        qev_hist = []
        pend_tail = [None]
        h_ost = (h_o0, h_o1)
        z_done = [0]
        rzA64 = ft[64:65, 5120:5632]
        rzB64 = ft[64:65, 5632:6144]
        rzA = ft[0:1, 5120:5632]
        rzB = ft[0:1, 5632:6144]
        bcA = ft[:, 4096:4608]
        bcB = ft[:, 4608:5120]

        def load_pair_w(pr, gate_ev, extra=False):
            slot = pr % 2
            satt = g_att0 if slot == 0 else g_att1
            P.wait("gpsimd", c_ev, gate_ev)
            if extra:
                gdma(pool2[:, W_V + slot * 4160:W_V + (slot + 1) * 4160],
                     vtmpl_d[:], satt)
                if slot == 0:
                    gdma(maskd, maskd_d[:], satt)
            for wi, wd in ((0, wq_d), (1, wk_d), (2, wv_d)):
                gdma(pool2[:, W_QKV + (slot * 3 + wi) * 1024:
                           W_QKV + (slot * 3 + wi + 1) * 1024].rearrange(
                               "p (k j) -> p k j", k=KC), wd[pr], satt)
            attw_done[pr] = n[satt.name]

        load_pair_w(0, n["c_ev"], extra=True)
        load_pair_w(1, n["c_ev"], extra=True)

        P.wait("tensor", c_wbe, n["c_wbe"])

        def emit_qkv_piece(pr, g):
            slot = pr % 2
            satt = g_att0 if slot == 0 else g_att1
            wq_s = pool2[:, W_QKV + (slot * 3 + 0) * 1024:
                         W_QKV + (slot * 3 + 1) * 1024]
            wk_s = pool2[:, W_QKV + (slot * 3 + 1) * 1024:
                         W_QKV + (slot * 3 + 2) * 1024]
            wv_s = pool2[:, W_QKV + (slot * 3 + 2) * 1024:
                         W_QKV + (slot * 3 + 3) * 1024]
            QBs = pool2[:, W_Q + slot * 4096:W_Q + (slot + 1) * 4096]
            KBs = pool2[:, W_K + slot * 4096:W_K + (slot + 1) * 4096]
            VAs = pool2[:, W_V + slot * 4160:W_V + slot * 4160 + 2080]
            VBs = pool2[:, W_V + slot * 4160 + 2080:W_V + (slot + 1) * 4160]
            P.wait("tensor", satt, attw_done[pr])
            P.wait("tensor", c_st, XLN_DONE)
            for which, wsb, dstb, bias in (("q", wq_s, QBs, bqs),
                                           ("k", wk_s, KBs, bks)):
                pq = p[6 + qg[0] % 2]
                if len(qev_hist) >= 2:
                    P.wait("tensor", c_ev, qev_hist[-2])
                for k in range(KC):
                    top(lambda t, k=k, g=g, wsb=wsb, pq=pq: t.matmul(
                        pq[:, :], wsb[:, k * 128:(k + 1) * 128],
                        pool1[:, k * seq + g * 512:k * seq + (g + 1) * 512],
                        start=(k == 0), stop=(k == KC - 1)),
                        c_mm if k == KC - 1 else None)
                P.wait("vector", c_mm, n["c_mm"])
                if which == "q":
                    vop(lambda v, g=g, pq=pq, dstb=dstb, bias=bias, pr=pr:
                        v.tensor_scalar(
                            out=dstb[:, g * 512:(g + 1) * 512], in0=pq[:, :],
                            scalar1=bias[:, pr:pr + 1], scalar2=0.125,
                            op0=ALU.add, op1=ALU.mult), c_ev)
                else:
                    vop(lambda v, g=g, pq=pq, dstb=dstb, bias=bias, pr=pr:
                        v.tensor_scalar(
                            out=dstb[:, g * 512:(g + 1) * 512], in0=pq[:, :],
                            scalar1=bias[:, pr:pr + 1], scalar2=None,
                            op0=ALU.add), c_ev)
                qg[0] += 1
                qev_hist.append(n["c_ev"])
            for tc in range(4 * g, 4 * g + 4):
                pq = p[6 + qg[0] % 2]
                if len(qev_hist) >= 2:
                    P.wait("tensor", c_ev, qev_hist[-2])
                for k in range(KC):
                    top(lambda t, k=k, tc=tc, wv_s=wv_s, pq=pq: t.matmul(
                        pq[:, 0:128],
                        pool1[:, k * seq + tc * 128:k * seq + (tc + 1) * 128],
                        wv_s[:, k * 128:(k + 1) * 128],
                        start=(k == 0), stop=(k == KC - 1)),
                        c_mm if k == KC - 1 else None)
                P.wait("vector", c_mm, n["c_mm"])
                P.wait("vector", satt, attw_done[pr])
                vop(lambda v, tc=tc, pq=pq, VAs=VAs, pr=pr: v.tensor_tensor(
                    out=VAs[:, tc * 65:tc * 65 + 64],
                    in0=pq[:, 0:64], in1=bvb[:, pr * 128:pr * 128 + 64],
                    op=ALU.add), c_ev)
                vop(lambda v, tc=tc, pq=pq, VBs=VBs, pr=pr: v.tensor_tensor(
                    out=VBs[:, tc * 65:tc * 65 + 64],
                    in0=pq[:, 64:128], in1=bvb[:, pr * 128 + 64:(pr + 1) * 128],
                    op=ALU.add), c_ev)
                qg[0] += 1
                qev_hist.append(n["c_ev"])
            if g == NT - 1:
                qkv_done[pr] = n["c_ev"]

        def emit_group_tail(pr, g, lastpar, exp_end):
            P.wait("vector", c_avb, n["c_avb"])
            P.wait("vector", h_o2, z_done[0])
            vop(lambda v: v.reciprocal(out=rzA64, in_=pp[2][64:65, 0:512]), c_nm)
            vop(lambda v: v.reciprocal(out=rzB64, in_=pp[2][64:65, 512:1024]),
                c_nm)
            P.wait("sync", c_nm, n["c_nm"])
            P.wait("sync", c_tp, n["c_tp"])
            sdma(rzA, rzA64, h_o2)
            sdma(rzB, rzB64, h_o2)
            z_done[0] = n[h_o2.name]
            zd_snap = z_done[0]
            par_g = gcnt[0] % 2
            gcnt[0] += 1
            h_os = h_ost[par_g]
            stgA = pool3[:, 14336 + par_g * 1024:14336 + par_g * 1024 + 512]
            stgB = pool3[:, 14336 + par_g * 1024 + 512:14336 + (par_g + 1) * 1024]

            def tail():
                P.wait("tensor", h_o2, zd_snap)
                P.wait("tensor", c_exp, exp_end)
                top(lambda t: t.matmul(pp[lastpar][:, 0:512], onesf[:], rzA,
                                       start=True, stop=True))
                top(lambda t: t.matmul(pp[lastpar][:, 512:1024], onesf[:], rzB,
                                       start=True, stop=True), c_tp)
                P.wait("vector", c_tp, n["c_tp"])
                vop(lambda v: v.tensor_copy(out=bcA, in_=pp[lastpar][:, 0:512]),
                    c_psc)
                vop(lambda v: v.tensor_copy(out=bcB, in_=pp[lastpar][:, 512:1024]),
                    c_psc)
                psc_req[lastpar] = n["c_psc"]
                P.wait("vector", h_os, n[h_os.name])
                vop(lambda v: v.tensor_tensor(
                    out=stgA[0:64, :], in0=pp[2][0:64, 0:512],
                    in1=bcA[0:64, :], op=ALU.mult), c_oev)
                vop(lambda v: v.tensor_tensor(
                    out=stgB[0:64, :], in0=pp[2][0:64, 512:1024],
                    in1=bcB[0:64, :], op=ALU.mult), c_oev)
                tt_done[0] = n["c_oev"]
                P.wait("sync", c_oev, n["c_oev"])
                sdma(os_d[pr][0:64, g * 512:(g + 1) * 512], stgA[0:64, :], h_os)
                sdma(os_d[pr][64:128, g * 512:(g + 1) * 512], stgB[0:64, :], h_os)
            return tail

        def flash_pair(pr):
            slot = pr % 2
            QBs = pool2[:, W_Q + slot * 4096:W_Q + (slot + 1) * 4096]
            KBs = pool2[:, W_K + slot * 4096:W_K + (slot + 1) * 4096]
            VAs = pool2[:, W_V + slot * 4160:W_V + slot * 4160 + 2080]
            VBs = pool2[:, W_V + slot * 4160 + 2080:W_V + (slot + 1) * 4160]
            P.wait("tensor", c_ev, qkv_done[pr])
            for g in range(NT):
                if pr + 1 < NPAIR:
                    emit_qkv_piece(pr + 1, g)
                    if g == NT - 1 and pr + 2 < NPAIR:
                        load_pair_w(pr + 2, qkv_done[pr])
                nch = 4 * (g + 1)
                gchunks = []

                def emit_S_exp(c):
                    par = scnt[0] % 2
                    if scnt[0] >= 2:
                        P.wait("tensor", c_exp, sexp_hist[scnt[0] - 2])
                    r = psc_req.get(par, 0)
                    if r:
                        P.wait("tensor", c_psc, r)
                    diag = (c >= 4 * g)
                    for hh in range(2):
                        top(lambda t, c=c, g=g, hh=hh, par=par, diag=diag:
                            t.matmul(
                                pp[par][:, hh * 512:(hh + 1) * 512],
                                KBs[hh * 64:(hh + 1) * 64, c * 128:(c + 1) * 128],
                                QBs[hh * 64:(hh + 1) * 64, g * 512:(g + 1) * 512],
                                start=True, stop=not diag,
                                tile_position=(hh * 64, 0)),
                            c_mm if (hh == 1 and not diag) else None)
                        if diag:
                            top(lambda t, c=c, g=g, hh=hh, par=par: t.matmul(
                                pp[par][:, hh * 512:(hh + 1) * 512], iden[:],
                                maskd[:, (c - 4 * g) * 512:(c - 4 * g + 1) * 512],
                                start=False, stop=True),
                                c_mm if hh == 1 else None)
                    P.wait("scalar", c_mm, n["c_mm"])
                    if scnt[0] >= 2:
                        P.wait("scalar", c_avb, sav_hist[scnt[0] - 2])
                    for hh in range(2):
                        P.op("scalar", lambda s, hh=hh, par=par: s.activation(
                            out=pool3[:, 12288 + par * 1024 + hh * 512:
                                      12288 + par * 1024 + (hh + 1) * 512],
                            in_=pp[par][:, hh * 512:(hh + 1) * 512],
                            func=AF.Exp), inc=[(c_exp, 1)])
                        n["c_exp"] += 1
                    sexp_hist.append(n["c_exp"])
                    gchunks.append((par, n["c_exp"]))
                    scnt[0] += 1

                def emit_AV(c):
                    par_c, expc = gchunks[c]
                    P.wait("tensor", c_exp, expc)
                    if c == 0:
                        P.wait("tensor", c_oev, tt_done[0])
                    for hh, Vb in ((0, VAs), (1, VBs)):
                        top(lambda t, c=c, hh=hh, par_c=par_c, Vb=Vb: t.matmul(
                            pp[2][0:65, hh * 512:(hh + 1) * 512],
                            Vb[:, c * 65:(c + 1) * 65],
                            pool3[:, 12288 + par_c * 1024 + hh * 512:
                                  12288 + par_c * 1024 + (hh + 1) * 512],
                            start=(c == 0), stop=(c == nch - 1)),
                            c_avb if hh == 1 else None)
                    sav_hist.append(n["c_avb"])

                for c in range(nch):
                    emit_S_exp(c)
                    if c == 0 and pend_tail[0] is not None:
                        pend_tail[0]()
                        pend_tail[0] = None
                    if c >= 1:
                        emit_AV(c - 1)
                emit_AV(nch - 1)
                pend_tail[0] = emit_group_tail(pr, g, gchunks[-1][0],
                                               n["c_exp"])

        for g in range(NT):
            emit_qkv_piece(0, g)
        for pr in range(NPAIR):
            flash_pair(pr)
        pend_tail[0]()
        pend_tail[0] = None

        for s_o in (h_o0, h_o1):
            P.wait("gpsimd", s_o, n[s_o.name])
        P.wait("gpsimd", c_avb, n["c_avb"])
        P.wait("gpsimd", c_oev, n["c_oev"])
        P.wait("gpsimd", c_tp, n["c_tp"])
        for pr in range(NPAIR):
            gdma(pool2[:, pr * 1024:(pr + 1) * 1024], wo_d[pr], g_wo)
        ATT_W = n["g_wo"]
        yb = pool3[:, 4096:8192]
        ysq = pool3[:, 8192:12288]
        xt2 = ft[:, FT_X:FT_X + 4096]
        P.wait("tensor", c_exp, n["c_exp"])
        P.wait("tensor", c_oev, n["c_oev"])
        P.wait("tensor", c_psc, n["c_psc"])
        P.wait("tensor", c_ev, n["c_ev"])
        for ts in range(NT):
            P.wait("gpsimd", c_ev, n["c_ev"])
            P.wait("gpsimd", c_st, n["c_st"])
            P.wait("gpsimd", h_xw, n["h_xw"])
            for pr in range(NPAIR):
                gdma(pool3[:, pr * 512:(pr + 1) * 512],
                     os_d[pr][:, ts * 512:(ts + 1) * 512], g_or)
            gdma(xt2.rearrange("p (k t) -> p k t", k=KC),
                 out_v[:, :, ts * 512:(ts + 1) * 512], g_xb)
            P.wait("tensor", g_wo, ATT_W)
            P.wait("tensor", g_or, n["g_or"])
            P.wait("tensor", c_st, n["c_st"])
            P.wait("vector", c_st, n["c_st"])
            for d in range(KC):
                P.wait("tensor", c_ev, max(0, n["c_ev"] - 1))
                for pr in range(NPAIR):
                    top(lambda t, d=d, pr=pr: t.matmul(
                        p[d % 2][:, :],
                        pool2[:, pr * 1024 + d * 128:pr * 1024 + (d + 1) * 128],
                        pool3[:, pr * 512:(pr + 1) * 512],
                        start=(pr == 0), stop=(pr == NPAIR - 1)),
                        c_mm if pr == NPAIR - 1 else None)
                P.wait("vector", c_mm, n["c_mm"])
                vop(lambda v, d=d: v.tensor_scalar(
                    out=yb[:, d * 512:(d + 1) * 512], in0=p[d % 2][:, :],
                    scalar1=bos[:, d:d + 1], scalar2=None, op0=ALU.add), c_ev)
            P.wait("scalar", c_ev, n["c_ev"])
            P.wait("scalar", c_mm, n["c_mm"])
            P.op("scalar", lambda s: s.activation(out=ysq, in_=yb, func=AF.Square),
                 inc=[(c_ln, 1)]); n["c_ln"] += 1
            P.wait("tensor", c_ln, n["c_ln"])
            P.wait("tensor", c_st, n["c_st"])
            stats_mms(yb, ysq)
            P.wait("vector", c_mm, n["c_mm"])
            r_mu, r_rs = ln_stats_and_rstd()
            P.wait("tensor", c_st, n["c_st"])
            bcast_mms(r_mu, r_rs)
            P.wait("vector", c_mm, n["c_mm"])
            P.wait("vector", g_xb, n["g_xb"])
            for k in range(KC):
                stg = ft[:, 3584:3584 + 512]
                vch(lambda v, k=k, stg=stg: v.tensor_tensor(
                    out=stg, in0=yb[:, k * 512:(k + 1) * 512], in1=p[5][:, :],
                    op=ALU.subtract))
                vch(lambda v, stg=stg: v.tensor_tensor(
                    out=stg, in0=stg, in1=p[6][:, :], op=ALU.mult))
                sch(lambda s, k=k, stg=stg: s.activation(
                    out=stg, in_=stg, func=AF.Identity,
                    scale=gams[:, k:k + 1], bias=bets[:, k:k + 1]))
                vch(lambda v, k=k, stg=stg: v.tensor_tensor(
                    out=xt2[:, k * 512:(k + 1) * 512],
                    in0=xt2[:, k * 512:(k + 1) * 512], in1=stg, op=ALU.add))
            P.wait("sync", c_st, n["c_st"])
            sdma(out_v[:, :, ts * 512:(ts + 1) * 512],
                 xt2.rearrange("p (k t) -> p k t", k=KC), h_xw)

    ffn_call(0, h_pre, 16, 0)
    XW0 = n["h_xw"]
    attention()
    ffn_call(1, h_xw, XW0, 16)
    P.wait("gpsimd", g_acca, n["g_acca"])
    P.wait("gpsimd", g_accb, n["g_accb"])
    P.wait("sync", h_xw, n["h_xw"])
    P.wait("sync", h_pre, PRE_H)
    P.wait("gpsimd", g_pre, PRE_G)

    P.check()

    blk = st.enter_context(nc.Block())

    @blk.gpsimd
    def _(g):
        P.emit("gpsimd", g)

    @blk.tensor
    def _(t):
        P.emit("tensor", t)

    @blk.scalar
    def _(s):
        P.emit("scalar", s)

    @blk.vector
    def _(v):
        P.emit("vector", v)

    @blk.sync
    def _(sy):
        P.emit("sync", sy)

    st.close()
    return nc


def _layouts(inputs):
    bf = ml_dtypes.bfloat16
    w1 = inputs["L0_ffn_w1"]
    w2 = inputs["L0_ffn_w2"]
    w1r = np.ascontiguousarray(
        w1.reshape(E, KC, 128, MC, 128).transpose(0, 2, 3, 1, 4)).astype(bf)
    w2r = np.ascontiguousarray(
        w2.reshape(E, MC, 128, D).transpose(0, 2, 1, 3)).astype(bf)
    b1r = np.ascontiguousarray(
        inputs["L0_ffn_b1"].reshape(E, MC, 128).transpose(2, 0, 1).reshape(128, E * MC))
    b2r = np.ascontiguousarray(inputs["L0_ffn_b2"]).astype(bf)
    gwT = np.ascontiguousarray(
        inputs["L0_ffn_gate_w"].T.reshape(KC, 128, 8)).astype(bf)
    gbb = np.ascontiguousarray(np.broadcast_to(inputs["L0_ffn_gate_b"], (128, 8)))

    def qk_r(w):
        return np.ascontiguousarray(
            w.reshape(KC, 128, NPAIR, 128).transpose(2, 1, 0, 3)).astype(bf)

    p_i = np.arange(128)[:, None]
    f_i = np.arange(512)[None, :]
    maskdb = np.concatenate(
        [np.where(cloc * 128 + p_i <= f_i, 0.0, -1e30).astype(np.float32)
         for cloc in range(4)], axis=1).astype(bf)
    vtmplb = np.zeros((128, 4160), dtype=np.float32)
    for h in range(2):
        for tc in range(32):
            vtmplb[:, h * 2080 + tc * 65 + 64] = 1.0
    common = dict(
        w1r=w1r, w2r=w2r, b1r=b1r, b2r=b2r, gwT=gwT, gbb=gbb,
        wqr=qk_r(inputs["L0_attn_Wq"]),
        wkr=qk_r(inputs["L0_attn_Wk"]),
        wvr=qk_r(inputs["L0_attn_Wv"]),
        wor=np.ascontiguousarray(
            inputs["L0_attn_Wo"].reshape(NPAIR, 128, D)).astype(bf),
        bqr=np.ascontiguousarray(inputs["L0_attn_bq"].reshape(NPAIR, 128).T),
        bkr=np.ascontiguousarray(inputs["L0_attn_bk"].reshape(NPAIR, 128).T),
        bvbr=np.ascontiguousarray(np.broadcast_to(
            inputs["L0_attn_bv"].reshape(1, NPAIR * 128),
            (128, NPAIR * 128))).astype(bf),
        bor=np.ascontiguousarray(inputs["L0_attn_bo"].reshape(KC, 128).T),
        gamr=np.ascontiguousarray(inputs["L0_attn_gamma"].reshape(KC, 128).T),
        betr=np.ascontiguousarray(inputs["L0_attn_beta"].reshape(KC, 128).T),
        idenb=np.eye(128, dtype=np.float32).astype(bf),
        maskdb=maskdb,
        vtmplb=vtmplb.astype(bf),
        onesb=np.ones((128, 1), dtype=np.float32).astype(bf),
        onesf=np.ones((1, 128), dtype=np.float32),
    )
    eselb = np.zeros((8, E * 128), dtype=np.float32)
    for e in range(E):
        eselb[e, e * 128:(e + 1) * 128] = 1.0
    common["eselb"] = eselb.astype(bf)
    return common


def kernel(**inputs):
    x = inputs["L0_attn_x"]
    seq = x.shape[1]
    common = _layouts(inputs)
    nc = build_program(seq)
    in_maps = []
    for c in range(x.shape[0]):
        m = dict(common)
        m["x"] = np.ascontiguousarray(x[c].T)
        in_maps.append(m)
    res = run_bass_kernel_spmd(nc, in_maps, core_ids=list(range(x.shape[0])))
    outs = [np.asarray(r["out"]).T for r in res.results]
    return np.stack(outs, axis=0).astype(np.float32)
